# version 13
# baseline (speedup 1.0000x reference)
import contextlib
import numpy as np
import ml_dtypes
import concourse.bass as bass
import concourse.mybir as mybir
from concourse.bass_utils import run_bass_kernel_spmd

F32 = mybir.dt.float32
BF16 = mybir.dt.bfloat16
ALU = mybir.AluOpType
AF = mybir.ActivationFunctionType
AX = mybir.AxisListType
MAGIC = float(np.float32(1.5 * 2 ** 23))
LN_EPS = 1e-5
RMS_EPS = 1e-6
Q_EPS = 1e-5


class CFG:
    def __init__(self, C=2048, H=5632, TOWN=2048, HALO=512,
                 F_KV=256, F_O=256, F_P0=512, THALF=1024, F_W3=256):
        self.C, self.H, self.TOWN, self.HALO = C, H, TOWN, HALO
        self.CCH, self.HCH = C // 128, H // 128
        self.TT = TOWN + HALO
        self.F_KV, self.F_O, self.F_P0 = F_KV, F_O, F_P0
        self.THALF, self.F_W3 = THALF, F_W3
        self.dbg = False
        assert self.TT % F_KV == 0 and self.TT % F_P0 == 0
        assert TOWN % F_O == 0 and TOWN % THALF == 0 and TOWN % F_W3 == 0
        assert HALO % F_KV == 0 and F_P0 <= 512 and F_O <= 512 and F_W3 <= 512
        assert THALF % 512 == 0 or THALF < 512


class Sems:
    NAMES = ["tt", "tv", "ts", "tse", "tg"]
    POOLS = {"din": 6, "dgp": 4, "dsc": 4}

    def __init__(self, nc, stack):
        self.s = {n: stack.enter_context(nc.semaphore(n)) for n in self.NAMES}
        self.c = {n: 0 for n in self.NAMES}
        for p, k in self.POOLS.items():
            for i in range(k):
                n = f"{p}{i}"
                self.s[n] = stack.enter_context(nc.semaphore(n))
                self.c[n] = 0
        self.cur = {p: 0 for p in self.POOLS}

    def fire(self, ins, name, n=1):
        self.c[name] += n
        ins.then_inc(self.s[name], n)
        return (name, self.c[name])

    def dma(self, eng, out, in_, pool):
        k = self.POOLS[pool]
        name = f"{pool}{self.cur[pool] % k}"
        self.cur[pool] += 1
        if self.c[name] > 0:
            eng.wait_ge(self.s[name], self.c[name])
        return self.fire(eng.dma_start(out=out, in_=in_), name, 16)

    def dma_seq(self, eng, pairs, pool):
        return [self.dma(eng, o, i, pool) for o, i in pairs]

    def wait(self, eng, evt):
        if evt is None:
            return
        if isinstance(evt, (list, tuple)) and evt and isinstance(evt[0],
                                                                 (list, tuple)):
            for e in evt:
                self.wait(eng, e)
            return
        if evt[1] > 0:
            eng.wait_ge(self.s[evt[0]], evt[1])

    def waits(self, eng, evts):
        for e in evts:
            self.wait(eng, e)

    def snap(self):
        return dict(self.c)

    def allev(self):
        return [(n, v) for n, v in self.c.items() if v > 0]

    def pool_snap(self, pool):
        return [(n, self.c[n]) for n in self.s if n.startswith(pool)]


def barrier(G):
    S, blk = G["S"], G["blk"]
    evs = S.allev()

    @blk.sync
    def _(e):
        S.waits(e, evs)

    @blk.tensor
    def _(e):
        S.waits(e, evs)

    @blk.vector
    def _(e):
        S.waits(e, evs)

    @blk.scalar
    def _(e):
        S.waits(e, evs)

    @blk.gpsimd
    def _(e):
        S.waits(e, evs)


def phase_sb(G, ph):
    def sbp(name, shape, dt=F32):
        return ph.enter_context(G["nc"].sbuf_tensor(name, shape, dt))
    return sbp


def pcf(sbuf_ap, F):
    return sbuf_ap.rearrange("p (c f) -> p c f", f=F)


def dpcf(dram_ap):
    return dram_ap.rearrange("c p f -> p c f")


def build(cfg):
    C, H, CCH, HCH = cfg.C, cfg.H, cfg.CCH, cfg.HCH
    TOWN, HALO, TT = cfg.TOWN, cfg.HALO, cfg.TT

    nc = bass.Bass("TRN2", target_bir_lowering=False)

    xT = nc.dram_tensor("xT", [CCH, 128, TT], F32, kind="ExternalInput").ap()
    wT = {k: nc.dram_tensor(k, [CCH, 128, C], F32, kind="ExternalInput").ap()
          for k in ("wkT", "wvT", "wrT", "woT")}
    w1T = nc.dram_tensor("w1T", [CCH, 128, H], F32, kind="ExternalInput").ap()
    w2T = nc.dram_tensor("w2T", [CCH, 128, H], F32, kind="ExternalInput").ap()
    w3T = nc.dram_tensor("w3T", [HCH, 128, C], F32, kind="ExternalInput").ap()
    vecs = {k: nc.dram_tensor(k, [128, CCH], F32, kind="ExternalInput").ap()
            for k in ("vtd", "vtf", "vmk", "vmv", "vmr")}
    flagD = nc.dram_tensor("flagD", [128, 1], F32, kind="ExternalInput").ap()
    onescbD = nc.dram_tensor("onescb", [128, 1], BF16, kind="ExternalInput").ap()
    onescfD = nc.dram_tensor("onescf", [128, 1], F32, kind="ExternalInput").ap()
    onesrfD = nc.dram_tensor("onesrf", [1, 128], F32, kind="ExternalInput").ap()
    identfD = nc.dram_tensor("identfD", [128, 128], F32, kind="ExternalInput").ap()
    outT = nc.dram_tensor("outT", [CCH, 128, TOWN], F32, kind="ExternalOutput").ap()

    LN1s = nc.dram_tensor("LN1s", [CCH, 128, TT], BF16, kind="Internal").ap()
    As = nc.dram_tensor("As", [CCH, 128, TT], F32, kind="Internal").ap()
    SRs = nc.dram_tensor("SRs", [CCH, 128, TOWN], BF16, kind="Internal").ap()
    NXs = nc.dram_tensor("NXs", [CCH, 128, TOWN + 1], F32, kind="Internal").ap()
    DXs = nc.dram_tensor("DXs", [CCH, 128, TOWN + 1], F32, kind="Internal").ap()
    AOLNs = nc.dram_tensor("AOLNs", [CCH, 128, TOWN], BF16, kind="Internal").ap()
    XQs = nc.dram_tensor("XQs", [CCH, 128, TOWN], BF16, kind="Internal").ap()
    Hs = nc.dram_tensor("Hs", [HCH, 128, TOWN], BF16, kind="Internal").ap()
    W1Q = nc.dram_tensor("W1Q", [HCH, 128, CCH * 128], BF16, kind="Internal").ap()
    W2Q = nc.dram_tensor("W2Q", [HCH, 128, CCH * 128], BF16, kind="Internal").ap()
    W3Q = nc.dram_tensor("W3Q", [CCH, 128, HCH * 128], BF16, kind="Internal").ap()
    DWNs = nc.dram_tensor("DWNs", [CCH, 128, TOWN], F32, kind="Internal").ap()
    X2s = nc.dram_tensor("X2s", [CCH, 128, TOWN], F32, kind="Internal").ap()

    st = contextlib.ExitStack()
    with st:
        S = Sems(nc, st)

        def sb(name, shape, dt=F32):
            return st.enter_context(nc.sbuf_tensor(name, shape, dt))

        ps = [st.enter_context(nc.psum_tensor(f"ps{i}", [128, 512], F32))
              for i in range(8)]

        lam = sb("lam", [128, CCH])
        eu = sb("eu", [128, CCH])
        utf = sb("utf", [128, CCH])
        k1c = sb("k1c", [128, CCH])
        mix = {m: sb("mix" + m, [128, CCH]) for m in "kvr"}
        flag = sb("flag", [128, 1])
        ones_cb = sb("ones_cb", [128, 1], BF16)
        ones_cf = sb("ones_cf", [128, 1], F32)
        ones_rf = sb("ones_rf", [1, 128], F32)
        identf = sb("identf", [128, 128], F32)
        wqs = sb("wqs", [128, 8])
        nthr = sb("nthr", [128, 3])

        blk = st.enter_context(nc.Block())
        G = dict(nc=nc, cfg=cfg, blk=blk, S=S, sb=sb, ps=ps, lam=lam, eu=eu,
                 utf=utf, k1c=k1c, mix=mix, flag=flag, ones_cb=ones_cb,
                 ones_cf=ones_cf,
                 ones_rf=ones_rf, identf=identf, wqs=wqs, nthr=nthr)

        @blk.sync
        def _(sy):
            loads = [(lam[:], vecs["vtd"]), (eu[:], vecs["vtf"]),
                     (utf[:], vecs["vtf"]),
                     (mix["k"][:], vecs["vmk"]), (mix["v"][:], vecs["vmv"]),
                     (mix["r"][:], vecs["vmr"]), (flag[:], flagD),
                     (ones_cb[:], onescbD), (ones_cf[:], onescfD),
                     (ones_rf[:], onesrfD), (identf[:], identfD)]
            G["ev_init_loads"] = S.dma_seq(sy, loads, "din")

        @blk.scalar
        def _(sc):
            S.wait(sc, G["ev_init_loads"])
            e1 = S.fire(sc.activation(lam[:], lam[:], AF.Exp), "ts")
            S.wait(sc, e1)
            e2 = S.fire(sc.activation(lam[:], lam[:], AF.Exp, scale=-1.0), "ts")
            e3 = S.fire(sc.activation(eu[:], eu[:], AF.Exp), "ts")
            e4 = S.fire(sc.activation(k1c[:], utf[:], AF.Exp, scale=-1.0), "ts")
            G["ev_lam"] = e4

        @blk.vector
        def _(v):
            S.wait(v, G["ev_lam"])
            G["ev_k1c"] = S.fire(v.tensor_tensor(k1c[:], k1c[:], lam[:],
                                                 ALU.subtract), "tv")

        wq_stack = contextlib.ExitStack()
        wq_sb = phase_sb(G, wq_stack)
        (means_task, n_means, finalize_means, tern_task,
         n_tern, wq_bufs) = make_wq_tasks(G, wq_sb, w1T, w2T, w3T,
                                          W1Q, W2Q, W3Q)
        sched = {"m": 0, "t": 0}

        def fill_means(k):
            def f():
                for _ in range(k):
                    if sched["m"] < n_means:
                        means_task(sched["m"]); sched["m"] += 1
            return f

        def fill_tern(k):
            def f():
                for _ in range(k):
                    if sched["t"] < n_tern:
                        tern_task(sched["t"]); sched["t"] += 1
            return f

        emit_p0(G, xT, LN1s, filler=fill_means(10))
        emit_proj(G, "K", LN1s, wT["wkT"], As, "k", AF.Exp, TT, odt=F32,
                  filler=fill_means(10), biasv=utf)
        emit_proj(G, "R", LN1s, wT["wrT"], SRs, "r", AF.Sigmoid, TOWN,
                  filler=fill_means(10))
        while sched["m"] < n_means:
            means_task(sched["m"]); sched["m"] += 1
        finalize_means()
        wq_stack.close()
        emit_phase_v(G, LN1s, wT["wvT"], As, NXs, DXs)

        def tern_ff(sbph):
            wq_bufs["wf"] = [sbph(f"tn_f{i}", [128, 1408]) for i in range(4)]
            wq_bufs["ta"] = [sbph(f"tn_a{i}", [128, 1408], BF16)
                             for i in range(2)]
            wq_bufs["tb"] = [sbph(f"tn_b{i}", [128, 1408], BF16)
                             for i in range(2)]
            wq_bufs["tq"] = [sbph(f"tn_q{i}", [128, 1408], BF16)
                             for i in range(2)]
            wq_bufs["NWF"] = 4
            wq_bufs["gponly"] = True
            return fill_tern(27)

        emit_phase_o(G, NXs, DXs, SRs, wT["woT"], AOLNs,
                     filler_factory=tern_ff)
        assert sched["t"] == n_tern, sched
        dscs = S.pool_snap("dsc")
        G["ev_w12q_done"] = dscs
        G["ev_w3q_done"] = dscs
        G["u2g_b"] = sb("u2g_b", [128, TOWN])
        G["u2u_b"] = sb("u2u_b", [128, TOWN])
        G["cs3_b"] = sb("cs3_b", [128, TOWN])
        G["u3_b"] = sb("u3_b", [128, TOWN])
        emit_ffnpre(G, xT, AOLNs, XQs, X2s)
        emit_gu(G, XQs, W1Q, W2Q, Hs)
        emit_w3(G, Hs, W3Q, X2s, outT, DWNs)

        if cfg.dbg:
            dbgmap = [("dLN1", LN1s), ("dA", As), ("dSR", SRs), ("dNX", NXs),
                      ("dDX", DXs), ("dAOLN", AOLNs), ("dXQ", XQs), ("dH", Hs),
                      ("dW3Q", W3Q), ("dDWN", DWNs)]
            dbg_t = {nm: nc.dram_tensor(nm, list(srcd.tensor.shape),
                                        srcd.tensor.dtype,
                                        kind="ExternalOutput").ap()
                     for nm, srcd in dbgmap}
            evs_all = S.allev()

            @blk.gpsimd
            def _(g):
                S.waits(g, evs_all)
                for nm, srcd in dbgmap:
                    S.dma(g, out=dbg_t[nm], in_=srcd, pool="dgp")

        @blk.sync
        def _(sy):
            S.waits(sy, S.allev())
    return nc


def ln_rows(G, F, ev_ready, psS, psQ, psM, psR, rows, C, ev_psfree=None):
    S, blk, ps = G["S"], G["blk"], G["ps"]
    ones_rf = G["ones_rf"]
    rowm, rowr, rt1, rt2 = rows["m"], rows["r"], rows["t1"], rows["t2"]

    @blk.vector
    def _(v):
        S.waits(v, ev_ready)
        v.tensor_scalar(rowm[:, 0:F], ps[psS][0:1, 0:F], 1.0 / C, None, ALU.mult)
        e1 = S.fire(v.tensor_scalar(rt1[:, 0:F], ps[psQ][0:1, 0:F], 1.0 / C,
                                    None, ALU.mult), "tv")
        G["_lr_psfree"] = e1
        S.wait(v, e1)
        e1b = S.fire(v.tensor_tensor(rt2[:, 0:F], rowm[:, 0:F], rowm[:, 0:F],
                                     ALU.mult), "tv")
        S.wait(v, e1b)
        e2 = S.fire(v.tensor_tensor(rt1[:, 0:F], rt1[:, 0:F], rt2[:, 0:F],
                                    ALU.subtract), "tv")
        S.wait(v, e2)
        G["_lr_var"] = S.fire(v.tensor_scalar(rt1[:, 0:F], rt1[:, 0:F], LN_EPS,
                                              None, ALU.add), "tv")

    @blk.scalar
    def _(sc):
        S.wait(sc, G["_lr_var"])
        e = S.fire(sc.activation(rt2[:, 0:F], rt1[:, 0:F], AF.Ln), "ts")
        S.wait(sc, e)
        G["_lr_sd"] = S.fire(sc.activation(rowr[:, 0:F], rt2[:, 0:F], AF.Exp,
                                           scale=-0.5), "ts")

    @blk.vector
    def _(v):
        S.wait(v, G["_lr_sd"])
        e1b = S.fire(v.tensor_tensor(rt2[:, 0:F], rowr[:, 0:F], rowr[:, 0:F],
                                     ALU.mult), "tv")
        S.wait(v, e1b)
        e2 = S.fire(v.tensor_tensor(rt2[:, 0:F], rt2[:, 0:F], rt1[:, 0:F],
                                    ALU.mult), "tv")
        S.wait(v, e2)
        e3 = S.fire(v.tensor_scalar(rt2[:, 0:F], rt2[:, 0:F], -0.5, 1.5,
                                    ALU.mult, ALU.add), "tv")
        S.wait(v, e3)
        G["_lr_rr"] = S.fire(v.tensor_tensor(rowr[:, 0:F], rowr[:, 0:F],
                                             rt2[:, 0:F], ALU.mult), "tv")

    @blk.tensor
    def _(t):
        S.wait(t, G["_lr_rr"])
        if ev_psfree is not None:
            S.wait(t, ev_psfree)
        t.matmul(ps[psM][:, 0:F], ones_rf[:], rowm[0:1, 0:F], start=True,
                 stop=True)
        G["_lr_bc"] = S.fire(t.matmul(ps[psR][:, 0:F], ones_rf[:],
                                      rowr[0:1, 0:F], start=True, stop=True),
                             "tt")
    return G["_lr_psfree"], G["_lr_bc"]


def emit_sums_sq(G, F, srcs, sq, ones_col, psS, psQ, ev_src, ev_sqfree,
                 ev_psfree, shadow=None):
    S, blk, ps = G["S"], G["blk"], G["ps"]
    NCH = len(srcs)
    nb = len(sq)
    tt0 = S.c["tt"]
    tv0 = [None]

    @blk.vector
    def _(v):
        if ev_src is not None:
            S.wait(v, ev_src)
        if ev_sqfree is not None:
            S.wait(v, ev_sqfree)
        ssrc = srcs
        if shadow is not None:
            for ch in range(NCH):
                ins = v.tensor_copy(shadow[:, ch * F:(ch + 1) * F], srcs[ch])
            e = S.fire(ins, "tv")
            v.wait_ge(S.s["tv"], e[1])
            ssrc = [shadow[:, ch * F:(ch + 1) * F] for ch in range(NCH)]
        G["_ss_src"] = ssrc
        tv0[0] = S.c["tv"]
        for ch in range(NCH):
            if ch >= nb:
                v.wait_ge(S.s["tt"], tt0 + ch - nb + 2)
            S.fire(v.tensor_tensor(sq[ch % nb][:, 0:F], ssrc[ch], ssrc[ch],
                                   ALU.mult), "tv")

    @blk.tensor
    def _(t):
        assert S.c["tt"] == tt0
        ssrc = G["_ss_src"]
        if ev_src is not None:
            S.wait(t, ev_src)
        if ev_psfree is not None:
            S.wait(t, ev_psfree)
        t.wait_ge(S.s["tv"], tv0[0])
        for ch in range(NCH):
            ins = t.matmul(ps[psS][0:1, 0:F], ones_col[:], ssrc[ch],
                           start=(ch == 0), stop=(ch == NCH - 1))
        ev_sum = S.fire(ins, "tt")
        for ch in range(NCH):
            t.wait_ge(S.s["tv"], tv0[0] + ch + 1)
            S.fire(t.matmul(ps[psQ][0:1, 0:F], ones_col[:], sq[ch % nb][:, 0:F],
                            start=(ch == 0), stop=(ch == NCH - 1)), "tt")
        G["_ss_ev"] = (ev_sum, ("tt", S.c["tt"]))
    return G["_ss_ev"]


def make_wq_tasks(G, sb, w1T, w2T, w3T, W1Q, W2Q, W3Q):
    cfg, S, blk, ps = G["cfg"], G["S"], G["blk"], G["ps"]
    C, H, CCH, HCH = cfg.C, cfg.H, cfg.CCH, cfg.HCH
    ones_rf, wqs, nthr, identf = (G["ones_rf"], G["wqs"], G["nthr"],
                                  G["identf"])
    QW = 1408
    NWF = 4
    wf = [sb(f"wq_f{i}", [128, QW]) for i in range(NWF)]
    red = sb("wq_red", [128, QW // 64])
    accv = sb("wq_acc", [128, 224])
    rowp = sb("wq_rowp", [1, 128])
    row = sb("wq_row", [1, 8])
    bufs = {"wf": wf, "NWF": NWF, "ta": None, "tb": None, "tq": None,
            "gponly": False}

    pieces_m, pieces_t = [], []
    npieces = {}
    for w, srcd, dstd, nrow, wid in (("w1", w1T, W1Q, CCH, H),
                                     ("w2", w2T, W2Q, CCH, H),
                                     ("w3", w3T, W3Q, HCH, C)):
        nq = wid // QW if wid % QW == 0 else 0
        if nq == 0:
            nq = 2
            qw = wid // 2
        else:
            qw = QW
        base = {"w1": 0, "w2": 64, "w3": 128}[w]
        k = 0
        for i in range(nrow):
            for q in range(wid // qw):
                pieces_m.append((w, srcd, i, q, qw, base + k))
                pieces_t.append((w, srcd, dstd, i, q, qw))
                k += 1
        npieces[w] = k
    streams = {}
    for p in pieces_t:
        streams.setdefault(p[0], []).append(p)
    merged, idx = [], {w: 0 for w in streams}
    total = len(pieces_t)
    for _ in range(total):
        w = min(streams, key=lambda s: idx[s] / len(streams[s]))
        merged.append(streams[w][idx[w]]); idx[w] += 1
    pieces_t = merged
    ev_mload, ev_mabs, ev_mpart = {}, {}, {}

    def fold_cols(v, buf, n, off=0):
        while n > 1:
            h = n // 2
            if n % 2:
                e = S.fire(v.tensor_tensor(buf[:, off:off + 1], buf[:, off:off + 1],
                                           buf[:, off + n - 1:off + n], ALU.add),
                           "tv")
                v.wait_ge(S.s["tv"], e[1])
            e = S.fire(v.tensor_tensor(buf[:, off:off + h], buf[:, off:off + h],
                                       buf[:, off + h:off + 2 * h], ALU.add),
                       "tv")
            v.wait_ge(S.s["tv"], e[1])
            n = h

    def means_task(j):
        w, srcd, i, q, qw, acol = pieces_m[j]

        @blk.gpsimd
        def _(g):
            if j >= NWF:
                S.wait(g, ev_mpart[j - NWF])
            ev_mload[j] = S.dma(g, out=wf[j % NWF][:, 0:qw],
                                in_=srcd[i, :, q * qw:(q + 1) * qw], pool="dgp")

        @blk.vector
        def _(v):
            S.wait(v, ev_mload[j])
            if j >= 1:
                S.wait(v, ev_mpart[j - 1])
            ng = qw // 64
            e0 = S.fire(v.tensor_reduce(
                red[:, 0:ng],
                wf[j % NWF][:, 0:qw].rearrange("p (g f) -> p g f", f=64),
                AX.X, ALU.add, apply_absolute_value=True), "tv")
            S.wait(v, e0)
            fold_cols(v, red, ng)
            ev_mpart[j] = S.fire(v.tensor_copy(accv[:, acol:acol + 1],
                                               red[:, 0:1]), "tv")

    def finalize_means():
        evw = {}
        nlast = ("tv", S.c["tv"])
        for a, (off, n) in enumerate(((0, npieces["w1"]), (64, npieces["w2"]),
                                      (128, npieces["w3"]))):
            @blk.vector
            def _(v, a=a, off=off, n=n):
                S.wait(v, ev_mpart[len(pieces_m) - 1])
                if a > 0:
                    S.wait(v, evw[a - 1])
                fold_cols(v, accv, n, off)
                evw[a] = S.fire(v.tensor_copy(accv[:, 220 + a:221 + a],
                                              accv[:, off:off + 1]), "tv")
        for a in range(3):
            @blk.tensor
            def _(t, a=a):
                S.wait(t, evw[2])
                if a > 0:
                    S.wait(t, G[f"_wqm_row{a-1}"])
                G[f"_wqm_tr{a}"] = S.fire(t.transpose(ps[5][0:1, 0:128],
                                                      accv[:, 220 + a:221 + a],
                                                      identf[:]), "tt")

            @blk.vector
            def _(v, a=a):
                S.wait(v, G[f"_wqm_tr{a}"])
                e0 = S.fire(v.tensor_copy(rowp[:, 0:128], ps[5][0:1, 0:128]),
                            "tv")
                S.wait(v, e0)
                n = 128
                while n > 1:
                    h = n // 2
                    e = S.fire(v.tensor_tensor(rowp[:, 0:h], rowp[:, 0:h],
                                               rowp[:, h:2 * h], ALU.add), "tv")
                    v.wait_ge(S.s["tv"], e[1])
                    n = h
                G[f"_wqm_row{a}"] = S.fire(v.tensor_copy(row[0:1, a:a + 1],
                                                         rowp[:, 0:1]), "tv")

        @blk.vector
        def _(v):
            S.wait(v, G["_wqm_row2"])
            r12 = float(np.float32(1.0) / np.float32(C * H))
            r3 = float(np.float32(1.0) / np.float32(H * C))
            v.tensor_scalar(row[0:1, 3:4], row[0:1, 0:1], r12, None, ALU.mult)
            v.tensor_scalar(row[0:1, 4:5], row[0:1, 1:2], r12, None, ALU.mult)
            e1 = S.fire(v.tensor_scalar(row[0:1, 5:6], row[0:1, 2:3], r3, None,
                                        ALU.mult), "tv")
            S.wait(v, e1)
            e2 = S.fire(v.tensor_scalar(row[0:1, 3:6], row[0:1, 3:6], Q_EPS,
                                        None, ALU.max), "tv")
            S.wait(v, e2)
            e3 = S.fire(v.reciprocal(row[0:1, 6:7], row[0:1, 5:6]), "tv")
            S.wait(v, e3)
            G["_wq_row"] = S.fire(v.tensor_scalar(row[0:1, 0:3], row[0:1, 3:6],
                                                  0.5, None, ALU.mult), "tv")

        @blk.tensor
        def _(t):
            S.wait(t, G["_wq_row"])
            G["_wq_bc"] = S.fire(t.matmul(ps[5][:, 0:7], ones_rf[:],
                                          row[0:1, 0:7], start=True, stop=True),
                                 "tt")

        @blk.vector
        def _(v):
            S.wait(v, G["_wq_bc"])
            e1 = S.fire(v.tensor_copy(wqs[:, 0:7], ps[5][:, 0:7]), "tv")
            S.wait(v, e1)
            G["ev_wqs"] = S.fire(v.tensor_scalar(nthr[:], wqs[:, 0:3], -1.0,
                                                 None, ALU.mult), "tv")
            v.wait_ge(S.s["tv"], S.c["tv"])
            G["ev_wqs2"] = S.fire(v.tensor_scalar(wqs[:, 3:5], wqs[:, 3:5], 0.5,
                                                  None, ALU.mult), "tv")

    ev_tload, ev_tcmp, ev_tq, ev_tst = {}, {}, {}, {}

    def tern_task(j):
        w, srcd, dstd, i, q, qw = pieces_t[j]
        nb_ = qw // 128
        wf_, nwf = bufs["wf"], bufs["NWF"]
        ta, tb, tq = bufs["ta"], bufs["tb"], bufs["tq"]

        def _ld(e):
            S.wait(e, G["ev_wqs2"])
            if j >= nwf:
                S.wait(e, ev_tcmp[j - nwf])
            ev_tload[j] = S.dma(e, out=wf_[j % nwf][:, 0:qw],
                                in_=srcd[i, :, q * qw:(q + 1) * qw],
                                pool="dgp")
        if bufs["gponly"] or j % 2 == 0:
            @blk.gpsimd
            def _(g):
                _ld(g)
        else:
            @blk.sync
            def _(sy):
                _ld(sy)

        if w != "w3":
            a = {"w1": 0, "w2": 1}[w]

            @blk.scalar
            def _(sc, a=a):
                S.wait(sc, ev_tload[j])
                if j >= 2:
                    S.wait(sc, ev_tq[j - 2])
                sc.activation(ta[j % 2][:, 0:qw], wf_[j % nwf][:, 0:qw],
                              AF.Sign, bias=nthr[:, a:a + 1])
                ev_tcmp[j] = S.fire(
                    sc.activation(tb[j % 2][:, 0:qw], wf_[j % nwf][:, 0:qw],
                                  AF.Sign, bias=wqs[:, a:a + 1]), "ts")

            @blk.vector
            def _(v):
                S.wait(v, ev_tcmp[j])
                if j >= 2:
                    S.wait(v, ev_tst[j - 2])
                ev_tq[j] = S.fire(
                    v.tensor_tensor(tq[j % 2][:, 0:qw], ta[j % 2][:, 0:qw],
                                    tb[j % 2][:, 0:qw], ALU.add), "tv")
        else:
            @blk.vector
            def _(v):
                S.wait(v, ev_tload[j])
                if j >= 2:
                    S.wait(v, ev_tq[j - 2])
                v.tensor_scalar(ta[j % 2][:, 0:qw], wf_[j % nwf][:, 0:qw],
                                wqs[:, 6:7], 0.5, ALU.mult, ALU.is_gt)
                ev_tcmp[j] = S.fire(
                    v.tensor_scalar(tb[j % 2][:, 0:qw], wf_[j % nwf][:, 0:qw],
                                    wqs[:, 6:7], -0.5, ALU.mult, ALU.is_lt),
                    "tv")
                S.wait(v, ev_tcmp[j])
                if j >= 2:
                    S.wait(v, ev_tst[j - 2])
                ev_tq[j] = S.fire(
                    v.tensor_tensor(tq[j % 2][:, 0:qw], ta[j % 2][:, 0:qw],
                                    tb[j % 2][:, 0:qw], ALU.subtract), "tv")

        @blk.scalar
        def _(sc):
            S.wait(sc, ev_tq[j])
            dv = dstd.rearrange("b p (c w) -> b p c w", w=128)[
                q * nb_:(q + 1) * nb_, :, i, :]
            ev_tst[j] = S.dma(sc, out=dv.rearrange("b p w -> p b w"),
                              in_=tq[j % 2][:, 0:qw].rearrange(
                                  "p (b w) -> p b w", w=128), pool="dsc")

    return (means_task, len(pieces_m), finalize_means, tern_task,
            len(pieces_t), bufs)


def emit_p0(G, xT, LN1s, filler=None):
    cfg, S, blk, sb, ps = G["cfg"], G["S"], G["blk"], G["sb"], G["ps"]
    _ph = contextlib.ExitStack()
    sb = phase_sb(G, _ph)
    barrier(G)
    C, CCH, TT, F = cfg.C, cfg.CCH, cfg.TT, cfg.F_P0
    NT = TT // F
    ones_cb, ones_rf = G["ones_cb"], G["ones_rf"]
    xf = [sb(f"p0xf{i}", [128, CCH * F]) for i in range(2)]
    xb = [sb("p0xb0", [128, CCH * F], BF16)]
    sq = [sb(f"p0sq{i}", [128, F], BF16) for i in range(2)]
    rows = {k: sb(f"p0_r{k}", [1, F]) for k in ("m", "r", "t1", "t2")}
    mb = sb("p0_mb", [128, F])
    rb = sb("p0_rb", [128, F])
    ln = [sb(f"p0_ln{i}", [128, CCH * F], BF16) for i in range(2)]
    ev_xf, ev_xb, ev_apply, ev_store, ev_mbrb = {}, {}, {}, {}, {}
    ev_sumdone = {}

    for j in range(NT):
        @blk.sync
        def _(sy, j=j):
            if j >= 2:
                S.wait(sy, ev_apply[j - 2])
            ev_xf[j] = S.dma(sy, 
                out=pcf(xf[j % 2][:], F),
                in_=dpcf(xT[:, :, j * F:(j + 1) * F]), pool="din")

        @blk.gpsimd
        def _(g, j=j):
            if j >= 1:
                S.wait(g, ev_sumdone[j - 1])
            ev_xb[j] = S.dma(g,
                out=pcf(xb[0][:], F),
                in_=dpcf(xT[:, :, j * F:(j + 1) * F]), pool="dgp")

        srcs = [xb[0][:, ch * F:(ch + 1) * F] for ch in range(CCH)]
        psfree = ev_mbrb.get(j - 1)
        ev_sum, ev_sq = emit_sums_sq(G, F, srcs, sq, ones_cb, 4, 5, ev_xb[j],
                                     ev_sumdone.get(j - 1), psfree)
        ev_sumdone[j] = ev_sq
        ev_psf, ev_bc = ln_rows(G, F, (ev_sum, ev_sq), 4, 5, 6, 7, rows, C,
                                ev_psfree=ev_mbrb.get(j - 1))

        @blk.vector
        def _(v, j=j, ev_bc=ev_bc):
            S.wait(v, ev_bc)
            v.tensor_copy(mb[:], ps[6][:, 0:F])
            ev_mbrb[j] = S.fire(v.tensor_copy(rb[:], ps[7][:, 0:F]), "tv")
            S.wait(v, ev_mbrb[j])
            S.wait(v, ev_xf[j])
            if j >= 2:
                S.wait(v, ev_store[j - 2])
            for ch in range(CCH):
                ins = v.tensor_tensor(ln[j % 2][:, ch * F:(ch + 1) * F],
                                      xf[j % 2][:, ch * F:(ch + 1) * F],
                                      mb[:], ALU.subtract)
            e1 = S.fire(ins, "tv")
            S.wait(v, e1)
            for ch in range(CCH):
                ins = v.tensor_tensor(ln[j % 2][:, ch * F:(ch + 1) * F],
                                      ln[j % 2][:, ch * F:(ch + 1) * F],
                                      rb[:], ALU.mult)
            ev_apply[j] = S.fire(ins, "tv")

        @blk.scalar
        def _(sc, j=j):
            S.wait(sc, ev_apply[j])
            ev_store[j] = S.dma(sc, 
                out=dpcf(LN1s[:, :, j * F:(j + 1) * F]),
                in_=pcf(ln[j % 2][:], F), pool="dsc")

        if filler is not None:
            filler()
    G["ev_p0_done"] = S.pool_snap("dsc")
    _ph.close()


def emit_proj(G, tag, LN1s, wTd, OUTs, mixkey, afunc, TOK, odt=BF16, filler=None,
              biasv=None):
    cfg, S, blk, sb, ps = G["cfg"], G["S"], G["blk"], G["sb"], G["ps"]
    _ph = contextlib.ExitStack()
    sb = phase_sb(G, _ph)
    barrier(G)
    C, CCH, F = cfg.C, cfg.CCH, cfg.F_KV
    NT = TOK // F
    goff = cfg.TT - TOK
    FP = F + 1
    mixv = G["mix"][mixkey]
    wsb = sb(f"{tag}_w", [128, CCH * C], BF16)
    lnb = [sb(f"{tag}_ln{i}", [128, CCH * FP], BF16) for i in range(2)]
    xm = [sb(f"{tag}_xm{i}", [128, CCH * F], BF16) for i in range(2)]
    ob = [sb(f"{tag}_o{i}", [128, CCH * F], odt) for i in range(2)]
    ev_w, ev_ln, ev_lerp, ev_store = {}, {}, {}, {}
    ev_mm_last, ev_evac = {}, {}

    @blk.gpsimd
    def _(g):
        ev_w[0] = S.dma_seq(g, [(wsb[:, ch * C:(ch + 1) * C], wTd[ch])
                                for ch in range(CCH)], "dgp")

    ts0 = [None]
    for j in range(NT):
        @blk.sync
        def _(sy, j=j):
            S.wait(sy, G["ev_p0_done"])
            if j >= 2:
                S.wait(sy, ev_lerp[j - 2])
            g0 = goff + j * F
            if g0 == 0:
                ev_ln[j] = S.dma(sy, 
                    out=pcf(lnb[j % 2][:], FP)[:, :, 1:FP],
                    in_=dpcf(LN1s[:, :, 0:F]), pool="din")
            else:
                ev_ln[j] = S.dma(sy, 
                    out=pcf(lnb[j % 2][:], FP),
                    in_=dpcf(LN1s[:, :, g0 - 1:g0 + F]), pool="din")

        @blk.vector
        def _(v, j=j):
            S.wait(v, ev_ln[j])
            if j == 0 and goff == 0:
                em = S.fire(v.memset(pcf(lnb[0][:], FP)[:, :, 0:1], 0.0), "tv")
                S.wait(v, em)
            if j >= 2:
                S.wait(v, ev_mm_last[j - 2])
            for ch in range(CCH):
                cur = lnb[j % 2][:, ch * FP + 1:ch * FP + 1 + F]
                sh = lnb[j % 2][:, ch * FP:ch * FP + F]
                ins = v.tensor_tensor(xm[j % 2][:, ch * F:(ch + 1) * F], cur, sh,
                                      ALU.subtract)
            e1 = S.fire(ins, "tv")
            S.wait(v, e1)
            for ch in range(CCH):
                sh = lnb[j % 2][:, ch * FP:ch * FP + F]
                ins = v.scalar_tensor_tensor(
                    xm[j % 2][:, ch * F:(ch + 1) * F],
                    xm[j % 2][:, ch * F:(ch + 1) * F],
                    mixv[:, ch:ch + 1], sh, ALU.mult, ALU.add)
            ev_lerp[j] = S.fire(ins, "tv")

        @blk.tensor
        def _(t, j=j):
            if j == 0:
                S.wait(t, ev_w[0])
                ts0[0] = S.c["tse"]
            S.wait(t, ev_lerp[j])
            for dch in range(CCH):
                grp = j * CCH + dch
                nbk = min(4, CCH)
                t.wait_ge(S.s["tse"], ts0[0] + max(0, grp - (nbk - 1)))
                for ch in range(CCH):
                    ins = t.matmul(ps[dch % 4][:, 0:F],
                                   wsb[:, ch * C + dch * 128:ch * C + (dch + 1) * 128],
                                   xm[j % 2][:, ch * F:(ch + 1) * F],
                                   start=(ch == 0), stop=(ch == CCH - 1))
                ev_evac[(j, dch)] = S.fire(ins, "tt")
            ev_mm_last[j] = ev_evac[(j, CCH - 1)]

        @blk.scalar
        def _(sc, j=j, afunc=afunc):
            for dch in range(CCH):
                S.wait(sc, ev_evac[(j, dch)])
                if dch == 0 and j >= 2:
                    S.wait(sc, ev_store[j - 2])
                kw = {} if biasv is None else {"bias": biasv[:, dch:dch + 1]}
                ins = sc.activation(ob[j % 2][:, dch * F:(dch + 1) * F],
                                    ps[dch % 4][:, 0:F], afunc, **kw)
                S.fire(ins, "tse")
            S.wait(sc, ("tse", S.c["tse"]))
            ev_store[j] = S.dma(sc,
                out=dpcf(OUTs[:, :, j * F:(j + 1) * F]),
                in_=pcf(ob[j % 2][:], F), pool="dsc")

        if filler is not None:
            filler()
    G[f"ev_{tag}_done"] = S.pool_snap("dsc")
    _ph.close()


def emit_phase_v(G, LN1s, wvTd, As, NXs, DXs):
    cfg, S, blk, sb, ps = G["cfg"], G["S"], G["blk"], G["sb"], G["ps"]
    _ph = contextlib.ExitStack()
    sb = phase_sb(G, _ph)
    barrier(G)
    C, CCH, F, TT, HALO = cfg.C, cfg.CCH, cfg.F_KV, cfg.TT, cfg.HALO
    NT = TT // F
    jb = HALO // F
    FP = F + 1
    lam, flag, mixv = G["lam"], G["flag"], G["mix"]["v"]
    wsb = sb("V_w", [128, CCH * C], BF16)
    lnb = [sb("V_ln0", [128, CCH * FP], BF16)]
    xm = [sb(f"V_xm{i}", [128, CCH * F], BF16) for i in range(2)]
    vb = sb("V_v", [128, CCH * F])
    pv = sb("V_pv", [128, CCH * F])
    ab = [sb("V_a0", [128, CCH * F], F32)]
    nsc = sb("V_nsc", [128, CCH * FP])
    dsc_ = sb("V_dsc", [128, CCH * FP])
    ev_ln, ev_a, ev_lerp, ev_mm_last, ev_evac = {}, {}, {}, {}, {}
    ev_pv, ev_scan, ev_nst, ev_dst = {}, {}, {}, {}

    @blk.gpsimd
    def _(g):
        G["_V_w"] = S.dma_seq(g, [(wsb[:, ch * C:(ch + 1) * C], wvTd[ch])
                                  for ch in range(CCH)], "dgp")

    ts0 = [None]
    for j in range(NT):
        @blk.sync
        def _(sy, j=j):
            S.wait(sy, G["ev_K_done"])
            if j >= 1:
                S.wait(sy, ev_lerp[j - 1])
            g0 = j * F
            if g0 == 0:
                ev_ln[j] = S.dma(sy, 
                    out=pcf(lnb[0][:], FP)[:, :, 1:FP],
                    in_=dpcf(LN1s[:, :, 0:F]), pool="din")
            else:
                ev_ln[j] = S.dma(sy, 
                    out=pcf(lnb[0][:], FP),
                    in_=dpcf(LN1s[:, :, g0 - 1:g0 + F]), pool="din")
            if j >= 1:
                S.wait(sy, ev_scan[j - 1])
            ev_a[j] = S.dma(sy,
                out=pcf(ab[0][:], F),
                in_=dpcf(As[:, :, j * F:(j + 1) * F]), pool="din")

        @blk.vector
        def _(v, j=j):
            S.wait(v, ev_ln[j])
            if j == 0:
                v.memset(pcf(lnb[0][:], FP)[:, :, 0:1], 0.0)
                v.memset(pcf(nsc[:], FP)[:, :, 0:1], 0.0)
                em = S.fire(v.memset(pcf(dsc_[:], FP)[:, :, 0:1], 0.0), "tv")
                S.wait(v, em)
            if j >= 2:
                S.wait(v, ev_mm_last[j - 2])
            for ch in range(CCH):
                cur = lnb[0][:, ch * FP + 1:ch * FP + 1 + F]
                sh = lnb[0][:, ch * FP:ch * FP + F]
                ins = v.tensor_tensor(xm[j % 2][:, ch * F:(ch + 1) * F], cur, sh,
                                      ALU.subtract)
            e1 = S.fire(ins, "tv")
            S.wait(v, e1)
            for ch in range(CCH):
                sh = lnb[0][:, ch * FP:ch * FP + F]
                ins = v.scalar_tensor_tensor(
                    xm[j % 2][:, ch * F:(ch + 1) * F],
                    xm[j % 2][:, ch * F:(ch + 1) * F],
                    mixv[:, ch:ch + 1], sh, ALU.mult, ALU.add)
            ev_lerp[j] = S.fire(ins, "tv")

        @blk.tensor
        def _(t, j=j):
            if j == 0:
                S.wait(t, G["_V_w"])
                ts0[0] = S.c["tse"]
            S.wait(t, ev_lerp[j])
            for dch in range(CCH):
                grp = j * CCH + dch
                nbk = min(4, CCH)
                t.wait_ge(S.s["tse"], ts0[0] + max(0, grp - (nbk - 1)))
                for ch in range(CCH):
                    ins = t.matmul(ps[dch % 4][:, 0:F],
                                   wsb[:, ch * C + dch * 128:ch * C + (dch + 1) * 128],
                                   xm[j % 2][:, ch * F:(ch + 1) * F],
                                   start=(ch == 0), stop=(ch == CCH - 1))
                ev_evac[(j, dch)] = S.fire(ins, "tt")
            ev_mm_last[j] = ev_evac[(j, CCH - 1)]

        @blk.scalar
        def _(sc, j=j):
            for dch in range(CCH):
                S.wait(sc, ev_evac[(j, dch)])
                if dch == 0 and j >= 1:
                    S.wait(sc, ev_pv[j - 1])
                S.fire(sc.copy(vb[:, dch * F:(dch + 1) * F], ps[dch % 4][:, 0:F]),
                       "tse")

        @blk.vector
        def _(v, j=j):
            v.wait_ge(S.s["tse"], ts0[0] + (j + 1) * CCH)
            S.wait(v, ev_a[j])
            if j >= 1:
                S.wait(v, ev_scan[j - 1])
            for ch in range(CCH):
                ins = v.tensor_tensor(pv[:, ch * F:(ch + 1) * F],
                                      ab[0][:, ch * F:(ch + 1) * F],
                                      vb[:, ch * F:(ch + 1) * F], ALU.mult)
            ev_pv[j] = S.fire(ins, "tv")
            S.wait(v, ev_pv[j])
            if j > 0:
                if j - 1 >= jb:
                    S.wait(v, ev_nst[j - 1])
                    S.wait(v, ev_dst[j - 1])
                ncf = pcf(nsc[:], FP)
                dcf = pcf(dsc_[:], FP)
                if j == jb:
                    v.tensor_scalar(ncf[:, :, 0:1], ncf[:, :, FP - 1:FP],
                                    flag[:, 0:1], None, ALU.mult)
                    ins = v.tensor_scalar(dcf[:, :, 0:1], dcf[:, :, FP - 1:FP],
                                          flag[:, 0:1], None, ALU.mult)
                else:
                    v.tensor_copy(ncf[:, :, 0:1], ncf[:, :, FP - 1:FP])
                    ins = v.tensor_copy(dcf[:, :, 0:1], dcf[:, :, FP - 1:FP])
                e2 = S.fire(ins, "tv")
                S.wait(v, e2)
            for ch in range(CCH):
                lam_b = lam[:, ch:ch + 1].broadcast_to((128, F))
                v.tensor_tensor_scan(nsc[:, ch * FP + 1:ch * FP + 1 + F], lam_b,
                                     pv[:, ch * F:(ch + 1) * F],
                                     nsc[:, ch * FP:ch * FP + 1],
                                     ALU.mult, ALU.add)
                ins = v.tensor_tensor_scan(dsc_[:, ch * FP + 1:ch * FP + 1 + F],
                                           lam_b, ab[0][:, ch * F:(ch + 1) * F],
                                           dsc_[:, ch * FP:ch * FP + 1],
                                           ALU.mult, ALU.add)
            ev_scan[j] = S.fire(ins, "tv")

        if j >= jb:
            @blk.scalar
            def _(sc, j=j):
                S.wait(sc, ev_scan[j])
                t1 = j * F - HALO
                w = FP if j == NT - 1 else F
                ev_nst[j] = S.dma(sc, 
                    out=dpcf(NXs[:, :, t1:t1 + w]),
                    in_=pcf(nsc[:], FP)[:, :, 0:w], pool="dsc")
                ev_dst[j] = S.dma(sc, 
                    out=dpcf(DXs[:, :, t1:t1 + w]),
                    in_=pcf(dsc_[:], FP)[:, :, 0:w], pool="dsc")
        else:
            ev_nst[j] = ev_dst[j] = None
    G["ev_V_done"] = S.pool_snap("dsc")
    _ph.close()


def emit_phase_o(G, NXs, DXs, SRs, woTd, AOLNs, filler_factory=None):
    cfg, S, blk, sb, ps = G["cfg"], G["S"], G["blk"], G["sb"], G["ps"]
    _ph = contextlib.ExitStack()
    sb = phase_sb(G, _ph)
    barrier(G)
    filler = filler_factory(sb) if filler_factory is not None else None
    C, CCH, F, TOWN = cfg.C, cfg.CCH, cfg.F_O, cfg.TOWN
    NT = TOWN // F
    FP = F + 1
    k1c = G["k1c"]
    FB = 128
    NSB = F // FB
    FPB = FB + 1
    wsb = sb("O_w", [128, CCH * C], BF16)
    nxw = [sb(f"O_nx{i}", [128, CCH * FPB]) for i in range(2)]
    dxw = [sb(f"O_dx{i}", [128, CCH * FPB]) for i in range(2)]
    srb = [sb(f"O_sr{i}", [128, CCH * F], BF16) for i in range(2)]
    bt1 = sb("O_t1", [128, CCH * FB])
    bt2 = sb("O_t2", [128, CCH * FB])
    rwk = [sb(f"O_rw{i}", [128, CCH * F], BF16) for i in range(2)]
    sq = [sb("O_sq0", [128, F], BF16)]
    rows = {k: sb(f"O_r{k}", [1, F]) for k in ("m", "r", "t1", "t2")}
    mb = sb("O_mb", [128, F])
    rb = sb("O_rb", [128, F])
    aol = [sb(f"O_aol{i}", [128, CCH * F], BF16) for i in range(2)]
    ev_in, ev_blend, ev_mm_last, ev_evac, ev_win, ev_bsub = {}, {}, {}, {}, {}, {}
    ev_aol, ev_store, ev_mbrb, ev_aodone, ev_sqdone = {}, {}, {}, {}, {}

    @blk.gpsimd
    def _(g):
        G["_O_w"] = S.dma_seq(g, [(wsb[:, ch * C:(ch + 1) * C], woTd[ch])
                                  for ch in range(CCH)], "dgp")

    ts0 = [None]
    post = {}
    for j in range(NT):
        @blk.sync
        def _(sy, j=j):
            S.wait(sy, G["ev_V_done"])
            S.wait(sy, G["ev_R_done"])
            if j >= 2:
                S.wait(sy, ev_blend[j - 2])
            ev_in[j] = S.dma(sy, out=pcf(srb[j % 2][:], F),
                             in_=dpcf(SRs[:, :, j * F:(j + 1) * F]), pool="din")
            for sbt in range(NSB):
                k = j * NSB + sbt
                if k >= 2:
                    S.wait(sy, ev_bsub[k - 2])
                g0 = j * F + sbt * FB
                ev_win[k] = S.dma_seq(sy, [
                    (pcf(nxw[k % 2][:], FPB), dpcf(NXs[:, :, g0:g0 + FPB])),
                    (pcf(dxw[k % 2][:], FPB), dpcf(DXs[:, :, g0:g0 + FPB]))],
                    "din")

        for sbt in range(NSB):
            k = j * NSB + sbt
            o = sbt * FB

            @blk.vector
            def _(v, j=j, sbt=sbt, k=k, o=o):
                S.wait(v, ev_win[k])
                S.wait(v, ev_in[j])
                if k >= 1:
                    S.wait(v, ev_bsub.get(k - 1, None))
                if j >= 2:
                    S.wait(v, ev_mm_last[j - 2])
                for ch in range(CCH):
                    v.scalar_tensor_tensor(bt1[:, ch * FB:(ch + 1) * FB],
                                           nxw[k % 2][:, ch * FPB:ch * FPB + FB],
                                           k1c[:, ch:ch + 1],
                                           nxw[k % 2][:, ch * FPB + 1:ch * FPB + 1 + FB],
                                           ALU.mult, ALU.add)
                    ins = v.scalar_tensor_tensor(bt2[:, ch * FB:(ch + 1) * FB],
                                                 dxw[k % 2][:, ch * FPB:ch * FPB + FB],
                                                 k1c[:, ch:ch + 1],
                                                 dxw[k % 2][:, ch * FPB + 1:ch * FPB + 1 + FB],
                                                 ALU.mult, ALU.add)
                e2 = S.fire(ins, "tv")
                S.wait(v, e2)
                e3 = S.fire(v.reciprocal(bt2[:], bt2[:]), "tv")
                S.wait(v, e3)
                e4 = S.fire(v.tensor_tensor(bt1[:], bt1[:], bt2[:], ALU.mult),
                            "tv")
                S.wait(v, e4)
                for ch in range(CCH):
                    ins = v.tensor_tensor(
                        rwk[j % 2][:, ch * F + o:ch * F + o + FB],
                        bt1[:, ch * FB:(ch + 1) * FB],
                        srb[j % 2][:, ch * F + o:ch * F + o + FB], ALU.mult)
                ev_bsub[k] = S.fire(ins, "tv")
        ev_blend[j] = ev_bsub[j * NSB + NSB - 1]

        @blk.tensor
        def _(t, j=j):
            if j == 0:
                S.wait(t, G["_O_w"])
                ts0[0] = S.c["tse"]
            S.wait(t, ev_blend[j])
            for dch in range(CCH):
                grp = j * CCH + dch
                nbk = min(4, CCH)
                t.wait_ge(S.s["tse"], ts0[0] + max(0, grp - (nbk - 1)))
                for ch in range(CCH):
                    ins = t.matmul(ps[dch % 4][:, 0:F],
                                   wsb[:, ch * C + dch * 128:ch * C + (dch + 1) * 128],
                                   rwk[j % 2][:, ch * F:(ch + 1) * F],
                                   start=(ch == 0), stop=(ch == CCH - 1))
                ev_evac[(j, dch)] = S.fire(ins, "tt")
            ev_mm_last[j] = ev_evac[(j, CCH - 1)]

        @blk.scalar
        def _(sc, j=j):
            for dch in range(CCH):
                S.wait(sc, ev_evac[(j, dch)])
                if dch == 0 and j >= 2:
                    S.wait(sc, ev_store[j - 2])
                S.fire(sc.copy(aol[j % 2][:, dch * F:(dch + 1) * F],
                               ps[dch % 4][:, 0:F]), "tse")
        ev_aoev = ("tse", S.c["tse"])

        def make_post(j, ev_aoev):
            def run():
                srcs = [aol[j % 2][:, ch * F:(ch + 1) * F]
                        for ch in range(CCH)]
                ev_sum, ev_sq = emit_sums_sq(G, F, srcs, sq, G["ones_cb"],
                                             4, 5, ev_aoev,
                                             ev_sqdone.get(j - 1),
                                             ev_mbrb.get(j - 1))
                ev_sqdone[j] = ev_sq
                ev_psf, ev_bc = ln_rows(G, F, (ev_sum, ev_sq), 4, 5, 6, 7,
                                        rows, C, ev_psfree=ev_mbrb.get(j - 1))

                @blk.vector
                def _(v, j=j, ev_bc=ev_bc):
                    S.wait(v, ev_bc)
                    v.tensor_copy(mb[:], ps[6][:, 0:F])
                    ev_mbrb[j] = S.fire(v.tensor_copy(rb[:], ps[7][:, 0:F]),
                                        "tv")
                    S.wait(v, ev_mbrb[j])
                    for ch in range(CCH):
                        ins = v.tensor_tensor(aol[j % 2][:, ch * F:(ch + 1) * F],
                                              aol[j % 2][:, ch * F:(ch + 1) * F],
                                              mb[:], ALU.subtract)
                    e1 = S.fire(ins, "tv")
                    S.wait(v, e1)
                    for ch in range(CCH):
                        ins = v.tensor_tensor(aol[j % 2][:, ch * F:(ch + 1) * F],
                                              aol[j % 2][:, ch * F:(ch + 1) * F],
                                              rb[:], ALU.mult)
                    ev_aol[j] = S.fire(ins, "tv")
                    ev_aodone[j] = ev_aol[j]

                @blk.scalar
                def _(sc, j=j):
                    S.wait(sc, ev_aol[j])
                    ev_store[j] = S.dma(sc,
                        out=dpcf(AOLNs[:, :, j * F:(j + 1) * F]),
                        in_=pcf(aol[j % 2][:], F), pool="dsc")
            return run

        if j >= 1:
            post[j - 1]()
        post[j] = make_post(j, ev_aoev)

        if filler is not None:
            filler()
    post[NT - 1]()
    G["ev_O_done"] = S.pool_snap("dsc")
    _ph.close()


def emit_ffnpre(G, xT, AOLNs, XQs, X2s):
    cfg, S, blk, sb, ps = G["cfg"], G["S"], G["blk"], G["sb"], G["ps"]
    _ph = contextlib.ExitStack()
    sb = phase_sb(G, _ph)
    barrier(G)
    C, CCH, TOWN, HALO = cfg.C, cfg.CCH, cfg.TOWN, cfg.HALO
    F = cfg.F_O
    NT = TOWN // F
    NB = F // 128
    wqs, ones_cf, ones_rf, identf = (G["wqs"], G["ones_cf"], G["ones_rf"],
                                     G["identf"])
    u2g_b, u2u_b = G["u2g_b"], G["u2u_b"]
    xf = [sb(f"FP_x{i}", [128, CCH * F]) for i in range(2)]
    aolb = [sb(f"FP_ao{i}", [128, CCH * F], BF16) for i in range(2)]
    x2 = sb("FP_x2", [128, CCH * F])
    zb = sb("FP_zb", [128, CCH * F])
    sq = [sb(f"FP_sq{i}", [128, F], BF16) for i in range(2)]
    shad = sb("FP_shad", [128, CCH * F], BF16)
    fmax = sb("FP_fmax", [128, F])
    fmin = sb("FP_fmin", [128, F])
    tmcol = sb("FP_tmcol", [128, 2])
    rows = {k: sb(f"FP_r{k}", [1, F]) for k in
            ("m", "r", "t1", "t2", "g", "mm", "cs", "mcs", "ug", "uu")}
    csb = sb("FP_csb", [128, F])
    mcsb = sb("FP_mcsb", [128, F])
    xq = [sb(f"FP_xq{i}", [128, CCH * F], BF16) for i in range(2)]
    ev_in, ev_x2, ev_fold, ev_q, ev_store = {}, {}, {}, {}, {}
    ev_psmax, ev_rowsdone, ev_bcread = {}, {}, {}
    ev_x2free, ev_bcfinal, ev_sqdone, ev_x2st = {}, {}, {}, {}
    tm_ev = {}

    for j in range(NT):
        @blk.sync
        def _(sy, j=j):
            S.wait(sy, G["ev_O_done"])
            if j >= 2:
                S.wait(sy, ev_q[j - 2])
            ev_in[j] = S.dma_seq(sy, [
                (pcf(xf[j % 2][:], F),
                 dpcf(xT[:, :, HALO + j * F:HALO + (j + 1) * F])),
                (pcf(aolb[j % 2][:], F),
                 dpcf(AOLNs[:, :, j * F:(j + 1) * F]))], "din")

        @blk.vector
        def _(v, j=j):
            S.wait(v, ev_in[j])
            if j >= 1:
                S.wait(v, ev_store[j - 1])
                S.wait(v, ev_x2st[j - 1])
            ev_x2[j] = S.fire(v.tensor_tensor(x2[:], xf[j % 2][:],
                                              aolb[j % 2][:], ALU.add), "tv")
            S.wait(v, ev_x2[j])
            if j >= 1:
                S.wait(v, tm_ev[(j - 1, NB - 1, "tr")])
            v.tensor_copy(fmax[:], x2[:, 0:F])
            e1 = S.fire(v.tensor_copy(fmin[:], x2[:, 0:F]), "tv")
            S.wait(v, e1)
            for ch in range(1, CCH):
                v.tensor_tensor(fmax[:], fmax[:], x2[:, ch * F:(ch + 1) * F],
                                ALU.max)
                ins = v.tensor_tensor(fmin[:], fmin[:],
                                      x2[:, ch * F:(ch + 1) * F], ALU.min)
            ev_fold[j] = S.fire(ins, "tv") if CCH > 1 else e1

        srcs = [x2[:, ch * F:(ch + 1) * F] for ch in range(CCH)]
        ev_sum, ev_sq = emit_sums_sq(G, F, srcs, sq, G["ones_cb"], 4, 5,
                                     ev_x2[j], ev_sqdone.get(j - 1),
                                     ev_bcfinal.get(j - 1), shadow=shad)
        ev_sqdone[j] = ev_sq

        for b in range(NB):
            @blk.tensor
            def _(t, j=j, b=b):
                S.wait(t, ev_fold[j])
                if b == 0:
                    if j >= 1:
                        S.wait(t, ev_bcfinal[j - 1])
                else:
                    S.wait(t, tm_ev[(j, b - 1, "red")])
                t.transpose(ps[6][:, 0:128], fmax[:, b * 128:(b + 1) * 128],
                            identf[:])
                tm_ev[(j, b, "tr")] = S.fire(
                    t.transpose(ps[6][:, 128:256], fmin[:, b * 128:(b + 1) * 128],
                                identf[:]), "tt")

            @blk.vector
            def _(v, j=j, b=b):
                S.wait(v, tm_ev[(j, b, "tr")])
                if b > 0:
                    S.wait(v, tm_ev[(j, b - 1, "row")])
                elif j >= 1:
                    S.wait(v, tm_ev[(j - 1, NB - 1, "row")])
                v.tensor_reduce(tmcol[:, 0:1], ps[6][:, 0:128], AX.X, ALU.max)
                tm_ev[(j, b, "red")] = S.fire(
                    v.tensor_reduce(tmcol[:, 1:2], ps[6][:, 128:256], AX.X,
                                    ALU.min), "tv")

            @blk.tensor
            def _(t, j=j, b=b):
                S.wait(t, tm_ev[(j, b, "red")])
                t.transpose(ps[7][0:1, b * 128:(b + 1) * 128],
                            tmcol[:, 0:1], identf[:])
                tm_ev[(j, b, "row")] = S.fire(
                    t.transpose(ps[7][0:1, F + b * 128:F + (b + 1) * 128],
                                tmcol[:, 1:2], identf[:]), "tt")
        ev_psmax[j] = tm_ev[(j, NB - 1, "row")]

        @blk.vector
        def _(v, j=j, ev_sum=ev_sum, ev_sq=ev_sq):
            S.waits(v, [ev_sum, ev_sq, ev_psmax[j]])
            v.tensor_scalar(rows["m"][:], ps[4][0:1, 0:F], 1.0 / C, None,
                            ALU.mult)
            v.tensor_scalar(rows["t1"][:], ps[5][0:1, 0:F], 1.0 / C, None,
                            ALU.mult)
            v.tensor_copy(rows["t2"][:], ps[7][0:1, 0:F])
            e0 = S.fire(v.tensor_copy(rows["mm"][:], ps[7][0:1, F:2 * F]), "tv")
            ev_bcread[j] = e0
            S.wait(v, e0)
            e0b = S.fire(v.tensor_tensor(rows["r"][:], rows["m"][:],
                                         rows["m"][:], ALU.mult), "tv")
            S.wait(v, e0b)
            e1 = S.fire(v.tensor_tensor(rows["t1"][:], rows["t1"][:],
                                        rows["r"][:], ALU.subtract), "tv")
            S.wait(v, e1)
            v.tensor_tensor(rows["t2"][:], rows["t2"][:], rows["m"][:],
                            ALU.subtract)
            e2 = S.fire(v.tensor_tensor(rows["mm"][:], rows["m"][:],
                                        rows["mm"][:], ALU.subtract), "tv")
            S.wait(v, e2)
            e3 = S.fire(v.tensor_tensor(rows["mm"][:], rows["mm"][:],
                                        rows["t2"][:], ALU.max), "tv")
            S.wait(v, e3)
            eC = S.fire(v.tensor_copy(rows["g"][:], rows["t1"][:]), "tv")
            S.wait(v, eC)
            G["_fp_am"] = S.fire(v.tensor_scalar(rows["t1"][:], rows["t1"][:],
                                                 LN_EPS, None, ALU.add), "tv")

        @blk.scalar
        def _(sc, j=j):
            S.wait(sc, G["_fp_am"])
            e = S.fire(sc.activation(rows["t2"][:], rows["t1"][:], AF.Ln), "ts")
            S.wait(sc, e)
            G["_fp_sd"] = S.fire(sc.activation(rows["r"][:], rows["t2"][:],
                                               AF.Exp, scale=-0.5), "ts")

        @blk.vector
        def _(v, j=j):
            S.wait(v, G["_fp_sd"])
            e1b = S.fire(v.tensor_tensor(rows["t2"][:], rows["r"][:],
                                         rows["r"][:], ALU.mult), "tv")
            S.wait(v, e1b)
            e2 = S.fire(v.tensor_tensor(rows["t2"][:], rows["t2"][:],
                                        rows["t1"][:], ALU.mult), "tv")
            S.wait(v, e2)
            e3 = S.fire(v.tensor_scalar(rows["t2"][:], rows["t2"][:], -0.5, 1.5,
                                        ALU.mult, ALU.add), "tv")
            S.wait(v, e3)
            e4 = S.fire(v.tensor_tensor(rows["r"][:], rows["r"][:],
                                        rows["t2"][:], ALU.mult), "tv")
            S.wait(v, e4)
            v.tensor_tensor(rows["t2"][:], rows["r"][:], rows["r"][:], ALU.mult)
            e5 = S.fire(v.engine_nop(), "tv")
            S.wait(v, e5)
            e7 = S.fire(v.tensor_tensor(rows["t2"][:], rows["t2"][:],
                                        rows["g"][:], ALU.mult), "tv")
            S.wait(v, e7)
            G["_fp_u"] = S.fire(v.tensor_scalar(rows["t2"][:], rows["t2"][:],
                                                RMS_EPS, None, ALU.add), "tv")

        @blk.scalar
        def _(sc, j=j):
            S.wait(sc, G["_fp_u"])
            e = S.fire(sc.activation(rows["t1"][:], rows["t2"][:], AF.Ln), "ts")
            S.wait(sc, e)
            G["_fp_sg"] = S.fire(sc.activation(rows["g"][:], rows["t1"][:],
                                               AF.Exp, scale=-0.5), "ts")

        @blk.vector
        def _(v, j=j):
            S.wait(v, G["_fp_sg"])
            e2 = S.fire(v.tensor_tensor(rows["t1"][:], rows["g"][:],
                                        rows["g"][:], ALU.mult), "tv")
            S.wait(v, e2)
            e3 = S.fire(v.tensor_tensor(rows["t1"][:], rows["t1"][:],
                                        rows["t2"][:], ALU.mult), "tv")
            S.wait(v, e3)
            e4 = S.fire(v.tensor_scalar(rows["t1"][:], rows["t1"][:], -0.5, 1.5,
                                        ALU.mult, ALU.add), "tv")
            S.wait(v, e4)
            e5 = S.fire(v.tensor_tensor(rows["g"][:], rows["g"][:],
                                        rows["t1"][:], ALU.mult), "tv")
            S.wait(v, e5)
            e6 = S.fire(v.tensor_tensor(rows["g"][:], rows["g"][:],
                                        rows["r"][:], ALU.mult), "tv")
            S.wait(v, e6)
            e7 = S.fire(v.tensor_tensor(rows["t1"][:], rows["mm"][:],
                                        rows["g"][:], ALU.mult), "tv")
            S.wait(v, e7)
            e8 = S.fire(v.tensor_scalar(rows["t1"][:], rows["t1"][:], Q_EPS,
                                        None, ALU.max), "tv")
            S.wait(v, e8)
            e9 = S.fire(v.reciprocal(rows["t2"][:], rows["t1"][:]), "tv")
            S.wait(v, e9)
            e10 = S.fire(v.tensor_scalar(rows["t2"][:], rows["t2"][:], 127.0,
                                         None, ALU.mult), "tv")
            S.wait(v, e10)
            e11 = S.fire(v.tensor_tensor(rows["cs"][:], rows["g"][:],
                                         rows["t2"][:], ALU.mult), "tv")
            S.wait(v, e11)
            e12 = S.fire(v.tensor_tensor(rows["mcs"][:], rows["m"][:],
                                         rows["cs"][:], ALU.mult), "tv")
            S.wait(v, e12)
            e13 = S.fire(v.tensor_scalar(rows["t1"][:], rows["t1"][:],
                                         1.0 / 127.0, None, ALU.mult), "tv")
            S.wait(v, e13)
            v.tensor_scalar(rows["ug"][:], rows["t1"][:], wqs[0:1, 3:4], None,
                            ALU.mult)
            G["_fp_rows"] = S.fire(v.tensor_scalar(rows["uu"][:], rows["t1"][:],
                                                   wqs[0:1, 4:5], None,
                                                   ALU.mult), "tv")
        ev_rowsdone[j] = G["_fp_rows"]

        @blk.tensor
        def _(t, j=j):
            S.wait(t, G["_fp_rows"])
            t.matmul(ps[6][:, 0:F], ones_rf[:], rows["cs"][0:1, :], start=True,
                     stop=True)
            t.matmul(ps[7][:, 0:F], ones_rf[:], rows["mcs"][0:1, :], start=True,
                     stop=True)
            t.matmul(ps[4][:, 0:F], ones_rf[:], rows["ug"][0:1, :], start=True,
                     stop=True)
            G["_fp_bc"] = S.fire(t.matmul(ps[5][:, 0:F], ones_rf[:],
                                          rows["uu"][0:1, :], start=True,
                                          stop=True), "tt")

        @blk.vector
        def _(v, j=j):
            S.wait(v, G["_fp_bc"])
            v.tensor_copy(csb[:], ps[6][:, 0:F])
            v.tensor_copy(mcsb[:], ps[7][:, 0:F])
            v.tensor_copy(u2g_b[:, j * F:(j + 1) * F], ps[4][:, 0:F])
            e0 = S.fire(v.tensor_copy(u2u_b[:, j * F:(j + 1) * F],
                                      ps[5][:, 0:F]), "tv")
            ev_bcfinal[j] = e0
            S.wait(v, e0)
            for ch in range(CCH):
                ins = v.tensor_tensor(zb[:, ch * F:(ch + 1) * F],
                                      x2[:, ch * F:(ch + 1) * F], csb[:],
                                      ALU.mult)
            e1 = S.fire(ins, "tv")
            S.wait(v, e1)
            for ch in range(CCH):
                ins = v.tensor_tensor(zb[:, ch * F:(ch + 1) * F],
                                      zb[:, ch * F:(ch + 1) * F], mcsb[:],
                                      ALU.subtract)
            e2 = S.fire(ins, "tv")
            S.wait(v, e2)
            if j >= 2:
                S.wait(v, ev_store[j - 2])
            ev_q[j] = S.fire(v.tensor_scalar(xq[j % 2][:], zb[:], MAGIC, -MAGIC,
                                             ALU.add, ALU.add), "tv")

        @blk.scalar
        def _(sc, j=j):
            S.wait(sc, ev_q[j])
            ev_x2st[j] = S.dma(sc, out=dpcf(X2s[:, :, j * F:(j + 1) * F]),
                               in_=pcf(x2[:], F), pool="dsc")
            ev_store[j] = S.dma(sc,
                out=dpcf(XQs[:, :, j * F:(j + 1) * F]),
                in_=pcf(xq[j % 2][:], F), pool="dsc")
    G["ev_FP_done"] = S.pool_snap("dsc")
    _ph.close()


def emit_gu(G, XQs, W1Q, W2Q, Hs):
    cfg, S, blk, sb, ps = G["cfg"], G["S"], G["blk"], G["sb"], G["ps"]
    _ph = contextlib.ExitStack()
    sb = phase_sb(G, _ph)
    barrier(G)
    C, H, CCH, HCH, TOWN = cfg.C, cfg.H, cfg.CCH, cfg.HCH, cfg.TOWN
    TH = cfg.THALF
    NHALF = TOWN // TH
    FQ = min(512, TH)
    NQ = TH // FQ
    wqs, ones_rf, identf = G["wqs"], G["ones_rf"], G["identf"]
    u2g_b, u2u_b = G["u2g_b"], G["u2u_b"]
    xqh = sb("GU_xq", [128, CCH * TH], BF16)
    w1b = [sb(f"GU_w1{i}", [128, CCH * 128], BF16) for i in range(2)]
    w2b = [sb(f"GU_w2{i}", [128, CCH * 128], BF16) for i in range(2)]
    gb = sb("GU_g", [128, TH])
    ub = sb("GU_u", [128, TH])
    hb = [sb(f"GU_h{i}", [128, TH], BF16) for i in range(2)]
    sqa = sb("GU_sqa", [128, TOWN])
    mxa = sb("GU_mxa", [128, TOWN])
    mna = sb("GU_mna", [128, TOWN])
    sqt = sb("GU_sqt", [128, TH])
    cs3_b, u3_b = G["cs3_b"], G["u3_b"]
    ev_xq, ev_w, ev_mm, ev_gu, ev_h, ev_hst, ev_stat = {}, {}, {}, {}, {}, {}, {}

    @blk.vector
    def _(v):
        v.memset(sqa[:], 0.0)
        v.memset(mxa[:], 0.0)
        G["_gu_z"] = S.fire(v.memset(mna[:], 0.0), "tv")

    for half in range(NHALF):
        th0 = half * TH

        @blk.sync
        def _(sy, half=half):
            S.wait(sy, G["ev_FP_done"])
            S.wait(sy, G["ev_w12q_done"])
            if half > 0:
                S.wait(sy, ev_mm[(half - 1, HCH - 1)])
            ev_xq[half] = S.dma(sy, 
                out=pcf(xqh[:], TH),
                in_=dpcf(XQs[:, :, th0:th0 + TH]), pool="din")

        for hb_i in range(HCH):
            @blk.sync
            def _(sy, half=half, hb_i=hb_i):
                if hb_i >= 2:
                    S.wait(sy, ev_mm[(half, hb_i - 2)])
                ev_w[(half, hb_i)] = S.dma_seq(sy, [
                    (w1b[hb_i % 2][:], W1Q[hb_i]),
                    (w2b[hb_i % 2][:], W2Q[hb_i])], "din")

            @blk.tensor
            def _(t, half=half, hb_i=hb_i):
                S.wait(t, ev_xq[half])
                S.wait(t, ev_w[(half, hb_i)])
                if hb_i >= 2:
                    S.wait(t, ev_gu[(half, hb_i - 2)])
                elif half > 0:
                    S.wait(t, ev_gu[(half - 1, HCH - 2 + hb_i)])
                par = hb_i % 2
                for gi, wb in ((0, w1b[par]), (1, w2b[par])):
                    for q in range(NQ):
                        bank = ps[par * 4 + gi * NQ + q]
                        for ch in range(CCH):
                            ins = t.matmul(
                                bank[:, 0:FQ],
                                wb[:, ch * 128:(ch + 1) * 128],
                                xqh[:, ch * TH + q * FQ:ch * TH + (q + 1) * FQ],
                                start=(ch == 0), stop=(ch == CCH - 1))
                ev_mm[(half, hb_i)] = S.fire(ins, "tt")

            @blk.vector
            def _(v, half=half, hb_i=hb_i):
                par = hb_i % 2
                S.wait(v, ev_mm[(half, hb_i)])
                if hb_i >= 1:
                    S.wait(v, ev_h[(half, hb_i - 1)])
                elif half >= 1:
                    S.wait(v, ev_h[(half - 1, HCH - 1)])
                for q in range(NQ):
                    v.tensor_tensor(gb[:, q * FQ:(q + 1) * FQ],
                                    ps[par * 4 + q][:, 0:FQ],
                                    u2g_b[:, th0 + q * FQ:th0 + (q + 1) * FQ],
                                    ALU.mult)
                    ins = v.tensor_tensor(
                        ub[:, q * FQ:(q + 1) * FQ],
                        ps[par * 4 + NQ + q][:, 0:FQ],
                        u2u_b[:, th0 + q * FQ:th0 + (q + 1) * FQ], ALU.mult)
                ev_gu[(half, hb_i)] = S.fire(ins, "tv")

            @blk.scalar
            def _(sc, half=half, hb_i=hb_i):
                S.wait(sc, ev_gu[(half, hb_i)])
                G["_gu_silu"] = S.fire(sc.activation(gb[:], gb[:], AF.Silu),
                                       "ts")

            @blk.vector
            def _(v, half=half, hb_i=hb_i):
                S.wait(v, G["_gu_silu"])
                if hb_i >= 2:
                    S.wait(v, ev_hst[(half, hb_i - 2)])
                elif half > 0:
                    S.wait(v, ev_hst[(half - 1, HCH - 2 + hb_i)])
                par = hb_i % 2
                ev_h[(half, hb_i)] = S.fire(
                    v.tensor_tensor(hb[par][:], gb[:], ub[:], ALU.mult), "tv")
                S.wait(v, ev_h[(half, hb_i)])
                v.tensor_tensor(sqt[:], hb[par][:], hb[par][:], ALU.mult)
                v.tensor_tensor(mxa[:, th0:th0 + TH], mxa[:, th0:th0 + TH],
                                hb[par][:], ALU.max)
                e1 = S.fire(v.tensor_tensor(mna[:, th0:th0 + TH],
                                            mna[:, th0:th0 + TH], hb[par][:],
                                            ALU.min), "tv")
                S.wait(v, e1)
                ev_stat[(half, hb_i)] = S.fire(
                    v.tensor_tensor(sqa[:, th0:th0 + TH], sqa[:, th0:th0 + TH],
                                    sqt[:], ALU.add), "tv")

            @blk.scalar
            def _(sc, half=half, hb_i=hb_i):
                S.wait(sc, ev_h[(half, hb_i)])
                ev_hst[(half, hb_i)] = S.dma(
                    sc, out=Hs[hb_i, :, th0:th0 + TH], in_=hb[hb_i % 2][:],
                    pool="dsc")

    NBT = TOWN // 128
    tmcol = sb("GU_tm", [128, 3])
    rows = sb("GU_rows", [1, TOWN])
    rowm2 = sb("GU_rowm2", [1, TOWN])
    rt1 = sb("GU_rt1", [1, TOWN])
    rt2 = sb("GU_rt2", [1, TOWN])
    ev_last_stat = ev_stat[(NHALF - 1, HCH - 1)]
    tm_ev = {}
    for b in range(NBT):
        @blk.tensor
        def _(t, b=b):
            S.wait(t, ev_last_stat)
            if b > 0:
                S.wait(t, tm_ev[(b - 1, "red")])
            t.transpose(ps[4][:, 0:128], sqa[:, b * 128:(b + 1) * 128], identf[:])
            t.transpose(ps[4][:, 128:256], mxa[:, b * 128:(b + 1) * 128],
                        identf[:])
            tm_ev[(b, "tr")] = S.fire(
                t.transpose(ps[4][:, 256:384], mna[:, b * 128:(b + 1) * 128],
                            identf[:]), "tt")

        @blk.vector
        def _(v, b=b):
            S.wait(v, tm_ev[(b, "tr")])
            if b > 0:
                S.wait(v, tm_ev[(b - 1, "rowcp")])
            v.tensor_reduce(tmcol[:, 0:1], ps[4][:, 0:128], AX.X, ALU.add)
            v.tensor_reduce(tmcol[:, 1:2], ps[4][:, 128:256], AX.X, ALU.max)
            tm_ev[(b, "red")] = S.fire(
                v.tensor_reduce(tmcol[:, 2:3], ps[4][:, 256:384], AX.X,
                                ALU.min), "tv")

        @blk.tensor
        def _(t, b=b):
            S.wait(t, tm_ev[(b, "red")])
            t.transpose(ps[5][0:1, 0:128], tmcol[:, 0:1], identf[:])
            t.transpose(ps[5][0:1, 128:256], tmcol[:, 1:2], identf[:])
            tm_ev[(b, "row")] = S.fire(
                t.transpose(ps[5][0:1, 256:384], tmcol[:, 2:3], identf[:]),
                "tt")

        @blk.vector
        def _(v, b=b):
            S.wait(v, tm_ev[(b, "row")])
            v.tensor_copy(rows[0:1, b * 128:(b + 1) * 128], ps[5][0:1, 0:128])
            v.tensor_copy(rowm2[0:1, b * 128:(b + 1) * 128],
                          ps[5][0:1, 128:256])
            tm_ev[(b, "rowcp")] = S.fire(
                v.tensor_copy(rt2[0:1, b * 128:(b + 1) * 128],
                              ps[5][0:1, 256:384]), "tv")

    @blk.vector
    def _(v):
        S.wait(v, tm_ev[(NBT - 1, "rowcp")])
        e00 = S.fire(v.tensor_scalar(rt2[:], rt2[:], -1.0, None, ALU.mult),
                     "tv")
        S.wait(v, e00)
        e01 = S.fire(v.tensor_tensor(rowm2[:], rowm2[:], rt2[:], ALU.max), "tv")
        S.wait(v, e01)
        e0 = S.fire(v.tensor_scalar(rows[:], rows[:], 1.0 / H, None, ALU.mult),
                    "tv")
        S.wait(v, e0)
        G["_g3_pre"] = S.fire(v.tensor_scalar(rt1[:], rows[:], RMS_EPS, None,
                                              ALU.add), "tv")

    @blk.scalar
    def _(sc):
        S.wait(sc, G["_g3_pre"])
        e = S.fire(sc.activation(rt2[:], rt1[:], AF.Ln), "ts")
        S.wait(sc, e)
        G["_g3_sq"] = S.fire(sc.activation(rows[:], rt2[:], AF.Exp,
                                           scale=-0.5), "ts")

    @blk.vector
    def _(v):
        S.wait(v, G["_g3_sq"])
        e2 = S.fire(v.tensor_tensor(rt2[:], rows[:], rows[:], ALU.mult), "tv")
        S.wait(v, e2)
        e3 = S.fire(v.tensor_tensor(rt2[:], rt2[:], rt1[:], ALU.mult), "tv")
        S.wait(v, e3)
        e4 = S.fire(v.tensor_scalar(rt2[:], rt2[:], -0.5, 1.5, ALU.mult,
                                    ALU.add), "tv")
        S.wait(v, e4)
        e5 = S.fire(v.tensor_tensor(rows[:], rows[:], rt2[:], ALU.mult), "tv")
        S.wait(v, e5)
        e6 = S.fire(v.tensor_tensor(rt1[:], rowm2[:], rows[:], ALU.mult), "tv")
        S.wait(v, e6)
        e7 = S.fire(v.tensor_scalar(rt1[:], rt1[:], Q_EPS, None, ALU.max), "tv")
        S.wait(v, e7)
        e8 = S.fire(v.reciprocal(rt2[:], rt1[:]), "tv")
        S.wait(v, e8)
        e9 = S.fire(v.tensor_scalar(rt2[:], rt2[:], 127.0, None, ALU.mult), "tv")
        S.wait(v, e9)
        e10 = S.fire(v.tensor_tensor(rt2[:], rows[:], rt2[:], ALU.mult), "tv")
        S.wait(v, e10)
        e11 = S.fire(v.tensor_scalar(rt1[:], rt1[:], 1.0 / 127.0, None,
                                     ALU.mult), "tv")
        S.wait(v, e11)
        G["_g3_rows"] = S.fire(v.tensor_scalar(rt1[:], rt1[:], wqs[0:1, 5:6],
                                               None, ALU.mult), "tv")

    NR = TOWN // min(512, TOWN)
    FR = min(512, TOWN)
    bc_ev = {}
    for r in range(NR):
        @blk.tensor
        def _(t, r=r):
            S.wait(t, G["_g3_rows"])
            if r > 0:
                S.wait(t, bc_ev[(r - 1, "cp")])
            t.matmul(ps[6][:, 0:FR], ones_rf[:], rt2[0:1, r * FR:(r + 1) * FR],
                     start=True, stop=True)
            bc_ev[(r, "mm")] = S.fire(
                t.matmul(ps[7][:, 0:FR], ones_rf[:],
                         rt1[0:1, r * FR:(r + 1) * FR], start=True, stop=True),
                "tt")

        @blk.vector
        def _(v, r=r):
            S.wait(v, bc_ev[(r, "mm")])
            v.tensor_copy(cs3_b[:, r * FR:(r + 1) * FR], ps[6][:, 0:FR])
            bc_ev[(r, "cp")] = S.fire(
                v.tensor_copy(u3_b[:, r * FR:(r + 1) * FR], ps[7][:, 0:FR]),
                "tv")
    G["ev_GU_done"] = [bc_ev[(NR - 1, "cp")]] + S.pool_snap("dsc")
    G["ev_GU_hst"] = ev_hst
    _ph.close()


def emit_w3(G, Hs, W3Q, X2s, outT, DWNs=None):
    cfg, S, blk, sb, ps = G["cfg"], G["S"], G["blk"], G["sb"], G["ps"]
    _ph = contextlib.ExitStack()
    sb = phase_sb(G, _ph)
    barrier(G)
    C, CCH, HCH, TOWN, HALO = cfg.C, cfg.CCH, cfg.HCH, cfg.TOWN, cfg.HALO
    F = cfg.F_W3
    NT = TOWN // F
    cs3_b, u3_b = G["cs3_b"], G["u3_b"]
    ones_cf, ones_rf = G["ones_cf"], G["ones_rf"]
    hqb = [sb(f"W3_hq{i}", [128, HCH * F], BF16) for i in range(2)]
    w3b = [sb(f"W3_w{i}", [128, HCH * 128], BF16) for i in range(2)]
    dwn = [sb(f"W3_d{i}", [128, CCH * F]) for i in range(2)]
    x2b = sb("W3_x2", [128, CCH * F])
    scr = [sb(f"W3_scr{i}", [128, F]) for i in range(2)]
    sq = [sb(f"W3_sq{i}", [128, F], BF16) for i in range(2)]
    shad = sb("W3_shad", [128, CCH * F], BF16)
    rows = {k: sb(f"W3_r{k}", [1, F]) for k in ("m", "r", "t1", "t2")}
    mb = sb("W3_mb", [128, F])
    rb = sb("W3_rb", [128, F])
    ob = [sb("W3_o0", [128, CCH * F])]
    ev_in, ev_q, ev_w, ev_mm, ev_x2in = {}, {}, {}, {}, {}
    ev_u3, ev_out, ev_store, ev_mbrb, ev_sqdone, ev_dwnfree = {}, {}, {}, {}, {}, {}

    ts0 = [None]
    post = {}
    for j in range(NT):
        @blk.sync
        def _(sy, j=j):
            S.wait(sy, G["ev_GU_done"])
            if j >= 2:
                S.wait(sy, ev_mm[(j - 2, CCH - 1)])
            ev_in[j] = S.dma(sy, out=pcf(hqb[j % 2][:], F),
                             in_=dpcf(Hs[:, :, j * F:(j + 1) * F]), pool="din")
            if j == 0:
                ev_x2in[0] = S.dma(sy, out=pcf(x2b[:], F),
                                   in_=dpcf(X2s[:, :, 0:F]), pool="din")

        @blk.vector
        def _(v, j=j):
            S.wait(v, ev_in[j])
            if j >= 2:
                S.wait(v, ev_mm[(j - 2, CCH - 1)])
            eprev = {}
            for hch in range(HCH):
                if hch >= 2:
                    S.wait(v, eprev[hch - 2])
                e1 = S.fire(v.tensor_tensor(scr[hch % 2][:],
                                            hqb[j % 2][:, hch * F:(hch + 1) * F],
                                            cs3_b[:, j * F:(j + 1) * F],
                                            ALU.mult), "tv")
                S.wait(v, e1)
                eprev[hch] = S.fire(
                    v.tensor_scalar(hqb[j % 2][:, hch * F:(hch + 1) * F],
                                    scr[hch % 2][:], MAGIC, -MAGIC, ALU.add,
                                    ALU.add), "tv")
            ev_q[j] = eprev[HCH - 1]

        for cb in range(CCH):
            k = j * CCH + cb

            @blk.gpsimd
            def _(g, j=j, cb=cb, k=k):
                S.wait(g, G["ev_w3q_done"])
                if k >= 2:
                    S.wait(g, ev_mm[(j, cb - 2)] if cb >= 2 else
                           ev_mm[(j - 1, CCH - 2 + cb)])
                ev_w[k] = S.dma(g,
                    out=w3b[k % 2][:], in_=W3Q[cb], pool="dgp")

            @blk.tensor
            def _(t, j=j, cb=cb, k=k):
                if k == 0:
                    ts0[0] = S.c["tse"]
                S.wait(t, ev_q[j])
                S.wait(t, ev_w[k])
                nbk = min(4, CCH)
                t.wait_ge(S.s["tse"], ts0[0] + max(0, k - (nbk - 1)))
                for hch in range(HCH):
                    ins = t.matmul(ps[cb % 4][:, 0:F],
                                   w3b[k % 2][:, hch * 128:(hch + 1) * 128],
                                   hqb[j % 2][:, hch * F:(hch + 1) * F],
                                   start=(hch == 0), stop=(hch == HCH - 1))
                ev_mm[(j, cb)] = S.fire(ins, "tt")

            @blk.vector
            def _(v, j=j, cb=cb, k=k):
                S.wait(v, ev_mm[(j, cb)])
                if cb == 0 and j >= 2:
                    S.wait(v, ev_dwnfree[j - 2])
                S.fire(v.tensor_tensor(dwn[j % 2][:, cb * F:(cb + 1) * F],
                                       ps[cb % 4][:, 0:F],
                                       u3_b[:, j * F:(j + 1) * F],
                                       ALU.mult), "tse")
        ev_u3[j] = ("tse", S.c["tse"])

        if G["cfg"].dbg:
            @blk.gpsimd
            def _(g, j=j):
                S.wait(g, ev_u3[j])
                S.dma(g, out=dpcf(DWNs[:, :, j * F:(j + 1) * F]),
                      in_=pcf(dwn[j % 2][:], F), pool="dgp")

        def make_post(j, ev_u3j):
            def run():
                srcs = [dwn[j % 2][:, ch * F:(ch + 1) * F]
                        for ch in range(CCH)]
                ev_sum, ev_sq = emit_sums_sq(G, F, srcs, sq, G["ones_cb"],
                                             4, 5, ev_u3j,
                                             ev_sqdone.get(j - 1),
                                             ev_mbrb.get(j - 1), shadow=shad)
                ev_sqdone[j] = ev_sq
                ev_psf, ev_bc = ln_rows(G, F, (ev_sum, ev_sq), 4, 5, 6, 7,
                                        rows, C, ev_psfree=ev_mbrb.get(j - 1))

                @blk.vector
                def _(v, j=j, ev_bc=ev_bc):
                    S.wait(v, ev_bc)
                    v.tensor_copy(mb[:], ps[6][:, 0:F])
                    ev_mbrb[j] = S.fire(v.tensor_copy(rb[:], ps[7][:, 0:F]),
                                        "tv")
                    S.wait(v, ev_mbrb[j])
                    if j >= 1:
                        S.wait(v, ev_store[j - 1])
                    for ch in range(CCH):
                        ins = v.tensor_tensor(ob[0][:, ch * F:(ch + 1) * F],
                                              dwn[j % 2][:, ch * F:(ch + 1) * F],
                                              mb[:], ALU.subtract)
                    e1 = S.fire(ins, "tv")
                    ev_dwnfree[j] = e1
                    S.wait(v, e1)
                    for ch in range(CCH):
                        ins = v.tensor_tensor(ob[0][:, ch * F:(ch + 1) * F],
                                              ob[0][:, ch * F:(ch + 1) * F],
                                              rb[:], ALU.mult)
                    e2 = S.fire(ins, "tv")
                    S.wait(v, e2)
                    S.wait(v, ev_x2in[j])
                    ev_out[j] = S.fire(v.tensor_tensor(ob[0][:], ob[0][:],
                                                       x2b[:], ALU.add), "tv")

                @blk.scalar
                def _(sc, j=j):
                    S.wait(sc, ev_out[j])
                    ev_store[j] = S.dma(sc,
                        out=dpcf(outT[:, :, j * F:(j + 1) * F]),
                        in_=pcf(ob[0][:], F), pool="dsc")

                if j + 1 < NT:
                    @blk.sync
                    def _(sy, j=j):
                        S.wait(sy, ev_out[j])
                        ev_x2in[j + 1] = S.dma(sy, out=pcf(x2b[:], F),
                            in_=dpcf(X2s[:, :, (j + 1) * F:(j + 2) * F]),
                            pool="din")
            return run

        if j >= 1:
            post[j - 1]()
        post[j] = make_post(j, ev_u3[j])
    post[NT - 1]()
    G["ev_out_done"] = ev_store[NT - 1]
    _ph.close()


def _shard_inputs(cfg, inputs):
    C, TOWN, HALO, CCH = cfg.C, cfg.TOWN, cfg.HALO, cfg.CCH
    x = np.asarray(inputs["L0_attn_x"], dtype=np.float32)
    B, T, _ = x.shape
    nsl = T // TOWN

    def t_w(w):
        wt = np.ascontiguousarray(np.asarray(w, dtype=np.float32).T)
        return np.ascontiguousarray(wt.reshape(wt.shape[0] // 128, 128,
                                               wt.shape[1]))

    def vec_c(v):
        return np.ascontiguousarray(
            np.asarray(v, dtype=np.float32).reshape(-1).reshape(CCH, 128).T)

    shared = dict(
        wkT=t_w(inputs["L0_attn_Wk"]), wvT=t_w(inputs["L0_attn_Wv"]),
        wrT=t_w(inputs["L0_attn_Wr"]), woT=t_w(inputs["L0_attn_Wo"]),
        w1T=t_w(inputs["L0_ffn_w1"]), w2T=t_w(inputs["L0_ffn_w2"]),
        w3T=t_w(inputs["L0_ffn_w3"]),
        vtd=vec_c(inputs["L0_attn_time_decay"]),
        vtf=vec_c(inputs["L0_attn_time_first"]),
        vmk=vec_c(inputs["L0_attn_time_mix_k"]),
        vmv=vec_c(inputs["L0_attn_time_mix_v"]),
        vmr=vec_c(inputs["L0_attn_time_mix_r"]),
        onescb=np.ones((128, 1), ml_dtypes.bfloat16),
        onescf=np.ones((128, 1), np.float32),
        onesrf=np.ones((1, 128), np.float32),
        identfD=np.eye(128, dtype=np.float32))
    in_maps = []
    ncores = B * nsl
    for c in range(ncores):
        b, s_ = c // nsl, c % nsl
        t0 = s_ * TOWN
        if s_ == 0:
            xs = np.concatenate([np.zeros((HALO, C), np.float32),
                                 x[b, 0:TOWN]], axis=0)
            fl = np.zeros((128, 1), np.float32)
        else:
            xs = x[b, t0 - HALO:t0 + TOWN]
            fl = np.ones((128, 1), np.float32)
        m = dict(shared)
        m["xT"] = np.ascontiguousarray(xs.T).reshape(CCH, 128, cfg.TT)
        m["flagD"] = fl
        in_maps.append(m)
    return in_maps


def kernel(**inputs):
    cfg = CFG()
    x = np.asarray(inputs["L0_attn_x"])
    B, T, C = x.shape
    in_maps = _shard_inputs(cfg, inputs)
    nc = build(cfg)
    res = run_bass_kernel_spmd(nc, in_maps, core_ids=list(range(len(in_maps))))
    out = np.empty((B, T, C), np.float32)
    nsl = T // cfg.TOWN
    for c in range(len(in_maps)):
        b, s_ = c // nsl, c % nsl
        o = np.asarray(res.results[c]["outT"])
        out[b, s_ * cfg.TOWN:(s_ + 1) * cfg.TOWN, :] = \
            o.reshape(cfg.C, cfg.TOWN).T
    return out


# version 14
# speedup vs baseline: 1.0376x; 1.0376x over previous
import contextlib
import numpy as np
import ml_dtypes
import concourse.bass as bass
import concourse.mybir as mybir
from concourse.bass_utils import run_bass_kernel_spmd

F32 = mybir.dt.float32
BF16 = mybir.dt.bfloat16
ALU = mybir.AluOpType
AF = mybir.ActivationFunctionType
AX = mybir.AxisListType
MAGIC = float(np.float32(1.5 * 2 ** 23))
LN_EPS = 1e-5
RMS_EPS = 1e-6
Q_EPS = 1e-5


class CFG:
    def __init__(self, C=2048, H=5632, TOWN=2048, HALO=512,
                 F_KV=256, F_O=256, F_P0=512, THALF=1024, F_W3=256):
        self.C, self.H, self.TOWN, self.HALO = C, H, TOWN, HALO
        self.CCH, self.HCH = C // 128, H // 128
        self.TT = TOWN + HALO
        self.F_KV, self.F_O, self.F_P0 = F_KV, F_O, F_P0
        self.THALF, self.F_W3 = THALF, F_W3
        self.dbg = False
        assert self.TT % F_KV == 0 and self.TT % F_P0 == 0
        assert TOWN % F_O == 0 and TOWN % THALF == 0 and TOWN % F_W3 == 0
        assert HALO % F_KV == 0 and F_P0 <= 512 and F_O <= 512 and F_W3 <= 512
        assert THALF % 512 == 0 or THALF < 512


class Sems:
    NAMES = ["tt", "tv", "ts", "tse", "tg"]
    POOLS = {"din": 6, "dgp": 4, "dsc": 4}

    def __init__(self, nc, stack):
        self.s = {n: stack.enter_context(nc.semaphore(n)) for n in self.NAMES}
        self.c = {n: 0 for n in self.NAMES}
        for p, k in self.POOLS.items():
            for i in range(k):
                n = f"{p}{i}"
                self.s[n] = stack.enter_context(nc.semaphore(n))
                self.c[n] = 0
        self.cur = {p: 0 for p in self.POOLS}

    def fire(self, ins, name, n=1):
        self.c[name] += n
        ins.then_inc(self.s[name], n)
        return (name, self.c[name])

    def dma(self, eng, out, in_, pool):
        k = self.POOLS[pool]
        name = f"{pool}{self.cur[pool] % k}"
        self.cur[pool] += 1
        if self.c[name] > 0:
            eng.wait_ge(self.s[name], self.c[name])
        return self.fire(eng.dma_start(out=out, in_=in_), name, 16)

    def dma_seq(self, eng, pairs, pool):
        return [self.dma(eng, o, i, pool) for o, i in pairs]

    def wait(self, eng, evt):
        if evt is None:
            return
        if isinstance(evt, (list, tuple)) and evt and isinstance(evt[0],
                                                                 (list, tuple)):
            for e in evt:
                self.wait(eng, e)
            return
        if evt[1] > 0:
            eng.wait_ge(self.s[evt[0]], evt[1])

    def waits(self, eng, evts):
        for e in evts:
            self.wait(eng, e)

    def snap(self):
        return dict(self.c)

    def allev(self):
        return [(n, v) for n, v in self.c.items() if v > 0]

    def pool_snap(self, pool):
        return [(n, self.c[n]) for n in self.s if n.startswith(pool)]


def barrier(G):
    S, blk = G["S"], G["blk"]
    evs = S.allev()

    @blk.sync
    def _(e):
        S.waits(e, evs)

    @blk.tensor
    def _(e):
        S.waits(e, evs)

    @blk.vector
    def _(e):
        S.waits(e, evs)

    @blk.scalar
    def _(e):
        S.waits(e, evs)

    @blk.gpsimd
    def _(e):
        S.waits(e, evs)


def phase_sb(G, ph):
    def sbp(name, shape, dt=F32):
        return ph.enter_context(G["nc"].sbuf_tensor(name, shape, dt))
    return sbp


def pcf(sbuf_ap, F):
    return sbuf_ap.rearrange("p (c f) -> p c f", f=F)


def dpcf(dram_ap):
    return dram_ap.rearrange("c p f -> p c f")


def build(cfg):
    C, H, CCH, HCH = cfg.C, cfg.H, cfg.CCH, cfg.HCH
    TOWN, HALO, TT = cfg.TOWN, cfg.HALO, cfg.TT

    nc = bass.Bass("TRN2", target_bir_lowering=False)

    xT = nc.dram_tensor("xT", [CCH, 128, TT], F32, kind="ExternalInput").ap()
    wT = {k: nc.dram_tensor(k, [CCH, 128, C], F32, kind="ExternalInput").ap()
          for k in ("wkT", "wvT", "wrT", "woT")}
    w1T = nc.dram_tensor("w1T", [CCH, 128, H], F32, kind="ExternalInput").ap()
    w2T = nc.dram_tensor("w2T", [CCH, 128, H], F32, kind="ExternalInput").ap()
    w3T = nc.dram_tensor("w3T", [HCH, 128, C], F32, kind="ExternalInput").ap()
    vecs = {k: nc.dram_tensor(k, [128, CCH], F32, kind="ExternalInput").ap()
            for k in ("vtd", "vtf", "vmk", "vmv", "vmr")}
    flagD = nc.dram_tensor("flagD", [128, 1], F32, kind="ExternalInput").ap()
    onescbD = nc.dram_tensor("onescb", [128, 1], BF16, kind="ExternalInput").ap()
    onescfD = nc.dram_tensor("onescf", [128, 1], F32, kind="ExternalInput").ap()
    onesrfD = nc.dram_tensor("onesrf", [1, 128], F32, kind="ExternalInput").ap()
    identfD = nc.dram_tensor("identfD", [128, 128], F32, kind="ExternalInput").ap()
    outT = nc.dram_tensor("outT", [CCH, 128, TOWN], F32, kind="ExternalOutput").ap()

    LN1s = nc.dram_tensor("LN1s", [CCH, 128, TT], BF16, kind="Internal").ap()
    As = nc.dram_tensor("As", [CCH, 128, TT], F32, kind="Internal").ap()
    SRs = nc.dram_tensor("SRs", [CCH, 128, TOWN], BF16, kind="Internal").ap()
    NXs = nc.dram_tensor("NXs", [CCH, 128, TOWN + 1], F32, kind="Internal").ap()
    DXs = nc.dram_tensor("DXs", [CCH, 128, TOWN + 1], F32, kind="Internal").ap()
    AOLNs = nc.dram_tensor("AOLNs", [CCH, 128, TOWN], BF16, kind="Internal").ap()
    XQs = nc.dram_tensor("XQs", [CCH, 128, TOWN], BF16, kind="Internal").ap()
    Hs = nc.dram_tensor("Hs", [HCH, 128, TOWN], BF16, kind="Internal").ap()
    W1Q = nc.dram_tensor("W1Q", [HCH, 128, CCH * 128], BF16, kind="Internal").ap()
    W2Q = nc.dram_tensor("W2Q", [HCH, 128, CCH * 128], BF16, kind="Internal").ap()
    W3Q = nc.dram_tensor("W3Q", [CCH, 128, HCH * 128], BF16, kind="Internal").ap()
    DWNs = nc.dram_tensor("DWNs", [CCH, 128, TOWN], F32, kind="Internal").ap()
    X2s = nc.dram_tensor("X2s", [CCH, 128, TOWN], F32, kind="Internal").ap()

    st = contextlib.ExitStack()
    with st:
        S = Sems(nc, st)

        def sb(name, shape, dt=F32):
            return st.enter_context(nc.sbuf_tensor(name, shape, dt))

        ps = [st.enter_context(nc.psum_tensor(f"ps{i}", [128, 512], F32))
              for i in range(8)]

        lam = sb("lam", [128, CCH])
        eu = sb("eu", [128, CCH])
        utf = sb("utf", [128, CCH])
        k1c = sb("k1c", [128, CCH])
        mix = {m: sb("mix" + m, [128, CCH]) for m in "kvr"}
        flag = sb("flag", [128, 1])
        ones_cb = sb("ones_cb", [128, 1], BF16)
        ones_cf = sb("ones_cf", [128, 1], F32)
        ones_rf = sb("ones_rf", [1, 128], F32)
        identf = sb("identf", [128, 128], F32)
        wqs = sb("wqs", [128, 8])
        nthr = sb("nthr", [128, 3])

        blk = st.enter_context(nc.Block())
        G = dict(nc=nc, cfg=cfg, blk=blk, S=S, sb=sb, ps=ps, lam=lam, eu=eu,
                 utf=utf, k1c=k1c, mix=mix, flag=flag, ones_cb=ones_cb,
                 ones_cf=ones_cf,
                 ones_rf=ones_rf, identf=identf, wqs=wqs, nthr=nthr)

        @blk.sync
        def _(sy):
            loads = [(lam[:], vecs["vtd"]), (eu[:], vecs["vtf"]),
                     (utf[:], vecs["vtf"]),
                     (mix["k"][:], vecs["vmk"]), (mix["v"][:], vecs["vmv"]),
                     (mix["r"][:], vecs["vmr"]), (flag[:], flagD),
                     (ones_cb[:], onescbD), (ones_cf[:], onescfD),
                     (ones_rf[:], onesrfD), (identf[:], identfD)]
            G["ev_init_loads"] = S.dma_seq(sy, loads, "din")

        @blk.scalar
        def _(sc):
            S.wait(sc, G["ev_init_loads"])
            e1 = S.fire(sc.activation(lam[:], lam[:], AF.Exp), "ts")
            S.wait(sc, e1)
            e2 = S.fire(sc.activation(lam[:], lam[:], AF.Exp, scale=-1.0), "ts")
            e3 = S.fire(sc.activation(eu[:], eu[:], AF.Exp), "ts")
            e4 = S.fire(sc.activation(k1c[:], utf[:], AF.Exp, scale=-1.0), "ts")
            G["ev_lam"] = e4

        @blk.vector
        def _(v):
            S.wait(v, G["ev_lam"])
            G["ev_k1c"] = S.fire(v.tensor_tensor(k1c[:], k1c[:], lam[:],
                                                 ALU.subtract), "tv")

        wq_stack = contextlib.ExitStack()
        wq_sb = phase_sb(G, wq_stack)
        (means_task, n_means, finalize_means, tern_task,
         n_tern, wq_bufs) = make_wq_tasks(G, wq_sb, w1T, w2T, w3T,
                                          W1Q, W2Q, W3Q)
        sched = {"m": 0, "t": 0}

        def fill_means(k):
            def f():
                for _ in range(k):
                    if sched["m"] < n_means:
                        means_task(sched["m"]); sched["m"] += 1
            return f

        def fill_tern(k):
            def f():
                for _ in range(k):
                    if sched["t"] < n_tern:
                        tern_task(sched["t"]); sched["t"] += 1
            return f

        emit_p0(G, xT, LN1s, filler=fill_means(10))
        emit_proj(G, "K", LN1s, wT["wkT"], As, "k", AF.Exp, TT, odt=F32,
                  filler=fill_means(10), biasv=utf)
        emit_proj(G, "R", LN1s, wT["wrT"], SRs, "r", AF.Sigmoid, TOWN,
                  filler=fill_means(10))
        while sched["m"] < n_means:
            means_task(sched["m"]); sched["m"] += 1
        finalize_means()
        wq_stack.close()
        emit_phase_v(G, LN1s, wT["wvT"], As, NXs, DXs)

        def tern_ff(sbph):
            wq_bufs["wf"] = [sbph(f"tn_f{i}", [128, 1408]) for i in range(4)]
            wq_bufs["ta"] = [sbph(f"tn_a{i}", [128, 1408], BF16)
                             for i in range(2)]
            wq_bufs["tb"] = [sbph(f"tn_b{i}", [128, 1408], BF16)
                             for i in range(2)]
            wq_bufs["tq"] = [sbph(f"tn_q{i}", [128, 1408], BF16)
                             for i in range(2)]
            wq_bufs["NWF"] = 4
            wq_bufs["gponly"] = True
            return fill_tern(27)

        emit_phase_o(G, NXs, DXs, SRs, wT["woT"], AOLNs,
                     filler_factory=tern_ff)
        assert sched["t"] == n_tern, sched
        dscs = S.pool_snap("dsc")
        G["ev_w12q_done"] = dscs
        G["ev_w3q_done"] = dscs
        G["u2g_b"] = sb("u2g_b", [128, TOWN])
        G["u2u_b"] = sb("u2u_b", [128, TOWN])
        G["cs3_b"] = sb("cs3_b", [128, TOWN])
        G["u3_b"] = sb("u3_b", [128, TOWN])
        emit_ffnpre(G, xT, AOLNs, XQs, X2s)
        emit_gu(G, XQs, W1Q, W2Q, Hs)
        emit_w3(G, Hs, W3Q, X2s, outT, DWNs)

        if cfg.dbg:
            dbgmap = [("dLN1", LN1s), ("dA", As), ("dSR", SRs), ("dNX", NXs),
                      ("dDX", DXs), ("dAOLN", AOLNs), ("dXQ", XQs), ("dH", Hs),
                      ("dW3Q", W3Q), ("dDWN", DWNs)]
            dbg_t = {nm: nc.dram_tensor(nm, list(srcd.tensor.shape),
                                        srcd.tensor.dtype,
                                        kind="ExternalOutput").ap()
                     for nm, srcd in dbgmap}
            evs_all = S.allev()

            @blk.gpsimd
            def _(g):
                S.waits(g, evs_all)
                for nm, srcd in dbgmap:
                    S.dma(g, out=dbg_t[nm], in_=srcd, pool="dgp")

        @blk.sync
        def _(sy):
            S.waits(sy, S.allev())
    return nc


def ln_rows(G, F, ev_ready, psS, psQ, psM, psR, rows, C, ev_psfree=None):
    S, blk, ps = G["S"], G["blk"], G["ps"]
    ones_rf = G["ones_rf"]
    rowm, rowr, rt1, rt2 = rows["m"], rows["r"], rows["t1"], rows["t2"]

    @blk.vector
    def _(v):
        S.waits(v, ev_ready)
        v.tensor_scalar(rowm[:, 0:F], ps[psS][0:1, 0:F], 1.0 / C, None, ALU.mult)
        e1 = S.fire(v.tensor_scalar(rt1[:, 0:F], ps[psQ][0:1, 0:F], 1.0 / C,
                                    None, ALU.mult), "tv")
        G["_lr_psfree"] = e1
        S.wait(v, e1)
        e1b = S.fire(v.tensor_tensor(rt2[:, 0:F], rowm[:, 0:F], rowm[:, 0:F],
                                     ALU.mult), "tv")
        S.wait(v, e1b)
        e2 = S.fire(v.tensor_tensor(rt1[:, 0:F], rt1[:, 0:F], rt2[:, 0:F],
                                    ALU.subtract), "tv")
        S.wait(v, e2)
        G["_lr_var"] = S.fire(v.tensor_scalar(rt1[:, 0:F], rt1[:, 0:F], LN_EPS,
                                              None, ALU.add), "tv")

    @blk.scalar
    def _(sc):
        S.wait(sc, G["_lr_var"])
        e = S.fire(sc.activation(rt2[:, 0:F], rt1[:, 0:F], AF.Ln), "ts")
        S.wait(sc, e)
        G["_lr_sd"] = S.fire(sc.activation(rowr[:, 0:F], rt2[:, 0:F], AF.Exp,
                                           scale=-0.5), "ts")

    @blk.vector
    def _(v):
        S.wait(v, G["_lr_sd"])
        e1b = S.fire(v.tensor_tensor(rt2[:, 0:F], rowr[:, 0:F], rowr[:, 0:F],
                                     ALU.mult), "tv")
        S.wait(v, e1b)
        e2 = S.fire(v.tensor_tensor(rt2[:, 0:F], rt2[:, 0:F], rt1[:, 0:F],
                                    ALU.mult), "tv")
        S.wait(v, e2)
        e3 = S.fire(v.tensor_scalar(rt2[:, 0:F], rt2[:, 0:F], -0.5, 1.5,
                                    ALU.mult, ALU.add), "tv")
        S.wait(v, e3)
        G["_lr_rr"] = S.fire(v.tensor_tensor(rowr[:, 0:F], rowr[:, 0:F],
                                             rt2[:, 0:F], ALU.mult), "tv")

    @blk.tensor
    def _(t):
        S.wait(t, G["_lr_rr"])
        if ev_psfree is not None:
            S.wait(t, ev_psfree)
        t.matmul(ps[psM][:, 0:F], ones_rf[:], rowm[0:1, 0:F], start=True,
                 stop=True)
        G["_lr_bc"] = S.fire(t.matmul(ps[psR][:, 0:F], ones_rf[:],
                                      rowr[0:1, 0:F], start=True, stop=True),
                             "tt")
    return G["_lr_psfree"], G["_lr_bc"]


def emit_sums_sq(G, F, srcs, sq, ones_col, psS, psQ, ev_src, ev_sqfree,
                 ev_psfree, shadow=None):
    S, blk, ps = G["S"], G["blk"], G["ps"]
    NCH = len(srcs)
    nb = len(sq)
    tt0 = S.c["tt"]
    tv0 = [None]

    @blk.vector
    def _(v):
        if ev_src is not None:
            S.wait(v, ev_src)
        if ev_sqfree is not None:
            S.wait(v, ev_sqfree)
        ssrc = srcs
        if shadow is not None:
            for ch in range(NCH):
                ins = v.tensor_copy(shadow[:, ch * F:(ch + 1) * F], srcs[ch])
            e = S.fire(ins, "tv")
            v.wait_ge(S.s["tv"], e[1])
            ssrc = [shadow[:, ch * F:(ch + 1) * F] for ch in range(NCH)]
        G["_ss_src"] = ssrc
        tv0[0] = S.c["tv"]
        for ch in range(NCH):
            if ch >= nb:
                v.wait_ge(S.s["tt"], tt0 + ch - nb + 2)
            S.fire(v.tensor_tensor(sq[ch % nb][:, 0:F], ssrc[ch], ssrc[ch],
                                   ALU.mult), "tv")

    @blk.tensor
    def _(t):
        assert S.c["tt"] == tt0
        ssrc = G["_ss_src"]
        if ev_src is not None:
            S.wait(t, ev_src)
        if ev_psfree is not None:
            S.wait(t, ev_psfree)
        t.wait_ge(S.s["tv"], tv0[0])
        for ch in range(NCH):
            ins = t.matmul(ps[psS][0:1, 0:F], ones_col[:], ssrc[ch],
                           start=(ch == 0), stop=(ch == NCH - 1))
        ev_sum = S.fire(ins, "tt")
        for ch in range(NCH):
            t.wait_ge(S.s["tv"], tv0[0] + ch + 1)
            S.fire(t.matmul(ps[psQ][0:1, 0:F], ones_col[:], sq[ch % nb][:, 0:F],
                            start=(ch == 0), stop=(ch == NCH - 1)), "tt")
        G["_ss_ev"] = (ev_sum, ("tt", S.c["tt"]))
    return G["_ss_ev"]


def make_wq_tasks(G, sb, w1T, w2T, w3T, W1Q, W2Q, W3Q):
    cfg, S, blk, ps = G["cfg"], G["S"], G["blk"], G["ps"]
    C, H, CCH, HCH = cfg.C, cfg.H, cfg.CCH, cfg.HCH
    ones_rf, wqs, nthr, identf = (G["ones_rf"], G["wqs"], G["nthr"],
                                  G["identf"])
    QW = 1408
    NWF = 4
    wf = [sb(f"wq_f{i}", [128, QW]) for i in range(NWF)]
    red = sb("wq_red", [128, QW // 64])
    accv = sb("wq_acc", [128, 224])
    rowp = sb("wq_rowp", [1, 128])
    row = sb("wq_row", [1, 8])
    bufs = {"wf": wf, "NWF": NWF, "ta": None, "tb": None, "tq": None,
            "gponly": False}

    pieces_m, pieces_t = [], []
    npieces = {}
    for w, srcd, dstd, nrow, wid in (("w1", w1T, W1Q, CCH, H),
                                     ("w2", w2T, W2Q, CCH, H),
                                     ("w3", w3T, W3Q, HCH, C)):
        nq = wid // QW if wid % QW == 0 else 0
        if nq == 0:
            nq = 2
            qw = wid // 2
        else:
            qw = QW
        base = {"w1": 0, "w2": 64, "w3": 128}[w]
        k = 0
        for i in range(nrow):
            for q in range(wid // qw):
                pieces_m.append((w, srcd, i, q, qw, base + k))
                pieces_t.append((w, srcd, dstd, i, q, qw))
                k += 1
        npieces[w] = k
    streams = {}
    for p in pieces_t:
        streams.setdefault(p[0], []).append(p)
    merged, idx = [], {w: 0 for w in streams}
    total = len(pieces_t)
    for _ in range(total):
        w = min(streams, key=lambda s: idx[s] / len(streams[s]))
        merged.append(streams[w][idx[w]]); idx[w] += 1
    pieces_t = merged
    ev_mload, ev_mabs, ev_mpart = {}, {}, {}

    def fold_cols(v, buf, n, off=0):
        while n > 1:
            h = n // 2
            if n % 2:
                e = S.fire(v.tensor_tensor(buf[:, off:off + 1], buf[:, off:off + 1],
                                           buf[:, off + n - 1:off + n], ALU.add),
                           "tv")
                v.wait_ge(S.s["tv"], e[1])
            e = S.fire(v.tensor_tensor(buf[:, off:off + h], buf[:, off:off + h],
                                       buf[:, off + h:off + 2 * h], ALU.add),
                       "tv")
            v.wait_ge(S.s["tv"], e[1])
            n = h

    def means_task(j):
        w, srcd, i, q, qw, acol = pieces_m[j]

        @blk.gpsimd
        def _(g):
            if j >= NWF:
                S.wait(g, ev_mpart[j - NWF])
            ev_mload[j] = S.dma(g, out=wf[j % NWF][:, 0:qw],
                                in_=srcd[i, :, q * qw:(q + 1) * qw], pool="dgp")

        @blk.vector
        def _(v):
            S.wait(v, ev_mload[j])
            if j >= 1:
                S.wait(v, ev_mpart[j - 1])
            ng = qw // 64
            e0 = S.fire(v.tensor_reduce(
                red[:, 0:ng],
                wf[j % NWF][:, 0:qw].rearrange("p (g f) -> p g f", f=64),
                AX.X, ALU.add, apply_absolute_value=True), "tv")
            S.wait(v, e0)
            fold_cols(v, red, ng)
            ev_mpart[j] = S.fire(v.tensor_copy(accv[:, acol:acol + 1],
                                               red[:, 0:1]), "tv")

    def finalize_means():
        evw = {}
        nlast = ("tv", S.c["tv"])
        for a, (off, n) in enumerate(((0, npieces["w1"]), (64, npieces["w2"]),
                                      (128, npieces["w3"]))):
            @blk.vector
            def _(v, a=a, off=off, n=n):
                S.wait(v, ev_mpart[len(pieces_m) - 1])
                if a > 0:
                    S.wait(v, evw[a - 1])
                fold_cols(v, accv, n, off)
                evw[a] = S.fire(v.tensor_copy(accv[:, 220 + a:221 + a],
                                              accv[:, off:off + 1]), "tv")
        for a in range(3):
            @blk.tensor
            def _(t, a=a):
                S.wait(t, evw[2])
                if a > 0:
                    S.wait(t, G[f"_wqm_row{a-1}"])
                G[f"_wqm_tr{a}"] = S.fire(t.transpose(ps[5][0:1, 0:128],
                                                      accv[:, 220 + a:221 + a],
                                                      identf[:]), "tt")

            @blk.vector
            def _(v, a=a):
                S.wait(v, G[f"_wqm_tr{a}"])
                e0 = S.fire(v.tensor_copy(rowp[:, 0:128], ps[5][0:1, 0:128]),
                            "tv")
                S.wait(v, e0)
                n = 128
                while n > 1:
                    h = n // 2
                    e = S.fire(v.tensor_tensor(rowp[:, 0:h], rowp[:, 0:h],
                                               rowp[:, h:2 * h], ALU.add), "tv")
                    v.wait_ge(S.s["tv"], e[1])
                    n = h
                G[f"_wqm_row{a}"] = S.fire(v.tensor_copy(row[0:1, a:a + 1],
                                                         rowp[:, 0:1]), "tv")

        @blk.vector
        def _(v):
            S.wait(v, G["_wqm_row2"])
            r12 = float(np.float32(1.0) / np.float32(C * H))
            r3 = float(np.float32(1.0) / np.float32(H * C))
            v.tensor_scalar(row[0:1, 3:4], row[0:1, 0:1], r12, None, ALU.mult)
            v.tensor_scalar(row[0:1, 4:5], row[0:1, 1:2], r12, None, ALU.mult)
            e1 = S.fire(v.tensor_scalar(row[0:1, 5:6], row[0:1, 2:3], r3, None,
                                        ALU.mult), "tv")
            S.wait(v, e1)
            e2 = S.fire(v.tensor_scalar(row[0:1, 3:6], row[0:1, 3:6], Q_EPS,
                                        None, ALU.max), "tv")
            S.wait(v, e2)
            e3 = S.fire(v.reciprocal(row[0:1, 6:7], row[0:1, 5:6]), "tv")
            S.wait(v, e3)
            G["_wq_row"] = S.fire(v.tensor_scalar(row[0:1, 0:3], row[0:1, 3:6],
                                                  0.5, None, ALU.mult), "tv")

        @blk.tensor
        def _(t):
            S.wait(t, G["_wq_row"])
            G["_wq_bc"] = S.fire(t.matmul(ps[5][:, 0:7], ones_rf[:],
                                          row[0:1, 0:7], start=True, stop=True),
                                 "tt")

        @blk.vector
        def _(v):
            S.wait(v, G["_wq_bc"])
            e1 = S.fire(v.tensor_copy(wqs[:, 0:7], ps[5][:, 0:7]), "tv")
            S.wait(v, e1)
            G["ev_wqs"] = S.fire(v.tensor_scalar(nthr[:], wqs[:, 0:3], -1.0,
                                                 None, ALU.mult), "tv")
            v.wait_ge(S.s["tv"], S.c["tv"])
            G["ev_wqs2"] = S.fire(v.tensor_scalar(wqs[:, 3:5], wqs[:, 3:5], 0.5,
                                                  None, ALU.mult), "tv")

    ev_tload, ev_tcmp, ev_tq, ev_tst = {}, {}, {}, {}

    def tern_task(j):
        w, srcd, dstd, i, q, qw = pieces_t[j]
        nb_ = qw // 128
        wf_, nwf = bufs["wf"], bufs["NWF"]
        ta, tb, tq = bufs["ta"], bufs["tb"], bufs["tq"]

        def _ld(e):
            S.wait(e, G["ev_wqs2"])
            if j >= nwf:
                S.wait(e, ev_tcmp[j - nwf])
            ev_tload[j] = S.dma(e, out=wf_[j % nwf][:, 0:qw],
                                in_=srcd[i, :, q * qw:(q + 1) * qw],
                                pool="dgp")
        if bufs["gponly"] or j % 2 == 0:
            @blk.gpsimd
            def _(g):
                _ld(g)
        else:
            @blk.sync
            def _(sy):
                _ld(sy)

        if w != "w3":
            a = {"w1": 0, "w2": 1}[w]

            @blk.scalar
            def _(sc, a=a):
                S.wait(sc, ev_tload[j])
                if j >= 2:
                    S.wait(sc, ev_tq[j - 2])
                sc.activation(ta[j % 2][:, 0:qw], wf_[j % nwf][:, 0:qw],
                              AF.Sign, bias=nthr[:, a:a + 1])
                ev_tcmp[j] = S.fire(
                    sc.activation(tb[j % 2][:, 0:qw], wf_[j % nwf][:, 0:qw],
                                  AF.Sign, bias=wqs[:, a:a + 1]), "ts")

            @blk.vector
            def _(v):
                S.wait(v, ev_tcmp[j])
                if j >= 2:
                    S.wait(v, ev_tst[j - 2])
                ev_tq[j] = S.fire(
                    v.tensor_tensor(tq[j % 2][:, 0:qw], ta[j % 2][:, 0:qw],
                                    tb[j % 2][:, 0:qw], ALU.add), "tv")
        else:
            @blk.vector
            def _(v):
                S.wait(v, ev_tload[j])
                if j >= 2:
                    S.wait(v, ev_tq[j - 2])
                v.tensor_scalar(ta[j % 2][:, 0:qw], wf_[j % nwf][:, 0:qw],
                                wqs[:, 6:7], 0.5, ALU.mult, ALU.is_gt)
                ev_tcmp[j] = S.fire(
                    v.tensor_scalar(tb[j % 2][:, 0:qw], wf_[j % nwf][:, 0:qw],
                                    wqs[:, 6:7], -0.5, ALU.mult, ALU.is_lt),
                    "tv")
                S.wait(v, ev_tcmp[j])
                if j >= 2:
                    S.wait(v, ev_tst[j - 2])
                ev_tq[j] = S.fire(
                    v.tensor_tensor(tq[j % 2][:, 0:qw], ta[j % 2][:, 0:qw],
                                    tb[j % 2][:, 0:qw], ALU.subtract), "tv")

        @blk.scalar
        def _(sc):
            S.wait(sc, ev_tq[j])
            dv = dstd.rearrange("b p (c w) -> b p c w", w=128)[
                q * nb_:(q + 1) * nb_, :, i, :]
            ev_tst[j] = S.dma(sc, out=dv.rearrange("b p w -> p b w"),
                              in_=tq[j % 2][:, 0:qw].rearrange(
                                  "p (b w) -> p b w", w=128), pool="dsc")

    return (means_task, len(pieces_m), finalize_means, tern_task,
            len(pieces_t), bufs)


def emit_p0(G, xT, LN1s, filler=None):
    cfg, S, blk, sb, ps = G["cfg"], G["S"], G["blk"], G["sb"], G["ps"]
    _ph = contextlib.ExitStack()
    sb = phase_sb(G, _ph)
    barrier(G)
    C, CCH, TT, F = cfg.C, cfg.CCH, cfg.TT, cfg.F_P0
    NT = TT // F
    ones_cb, ones_rf = G["ones_cb"], G["ones_rf"]
    xf = [sb(f"p0xf{i}", [128, CCH * F]) for i in range(2)]
    xb = [sb("p0xb0", [128, CCH * F], BF16)]
    sq = [sb(f"p0sq{i}", [128, F], BF16) for i in range(2)]
    rows = {k: sb(f"p0_r{k}", [1, F]) for k in ("m", "r", "t1", "t2")}
    mb = sb("p0_mb", [128, F])
    rb = sb("p0_rb", [128, F])
    ln = [sb(f"p0_ln{i}", [128, CCH * F], BF16) for i in range(2)]
    ev_xf, ev_xb, ev_apply, ev_store, ev_mbrb = {}, {}, {}, {}, {}
    ev_sumdone = {}

    for j in range(NT):
        @blk.sync
        def _(sy, j=j):
            if j >= 2:
                S.wait(sy, ev_apply[j - 2])
            ev_xf[j] = S.dma(sy, 
                out=pcf(xf[j % 2][:], F),
                in_=dpcf(xT[:, :, j * F:(j + 1) * F]), pool="din")

        @blk.gpsimd
        def _(g, j=j):
            if j >= 1:
                S.wait(g, ev_sumdone[j - 1])
            ev_xb[j] = S.dma(g,
                out=pcf(xb[0][:], F),
                in_=dpcf(xT[:, :, j * F:(j + 1) * F]), pool="dgp")

        srcs = [xb[0][:, ch * F:(ch + 1) * F] for ch in range(CCH)]
        psfree = ev_mbrb.get(j - 1)
        ev_sum, ev_sq = emit_sums_sq(G, F, srcs, sq, ones_cb, 4, 5, ev_xb[j],
                                     ev_sumdone.get(j - 1), psfree)
        ev_sumdone[j] = ev_sq
        ev_psf, ev_bc = ln_rows(G, F, (ev_sum, ev_sq), 4, 5, 6, 7, rows, C,
                                ev_psfree=ev_mbrb.get(j - 1))

        @blk.vector
        def _(v, j=j, ev_bc=ev_bc):
            S.wait(v, ev_bc)
            v.tensor_copy(mb[:], ps[6][:, 0:F])
            ev_mbrb[j] = S.fire(v.tensor_copy(rb[:], ps[7][:, 0:F]), "tv")
            S.wait(v, ev_mbrb[j])
            S.wait(v, ev_xf[j])
            if j >= 2:
                S.wait(v, ev_store[j - 2])
            for ch in range(CCH):
                ins = v.tensor_tensor(ln[j % 2][:, ch * F:(ch + 1) * F],
                                      xf[j % 2][:, ch * F:(ch + 1) * F],
                                      mb[:], ALU.subtract)
            e1 = S.fire(ins, "tv")
            S.wait(v, e1)
            for ch in range(CCH):
                ins = v.tensor_tensor(ln[j % 2][:, ch * F:(ch + 1) * F],
                                      ln[j % 2][:, ch * F:(ch + 1) * F],
                                      rb[:], ALU.mult)
            ev_apply[j] = S.fire(ins, "tv")

        @blk.scalar
        def _(sc, j=j):
            S.wait(sc, ev_apply[j])
            ev_store[j] = S.dma(sc, 
                out=dpcf(LN1s[:, :, j * F:(j + 1) * F]),
                in_=pcf(ln[j % 2][:], F), pool="dsc")

        if filler is not None:
            filler()
    G["ev_p0_done"] = S.pool_snap("dsc")
    _ph.close()


def emit_proj(G, tag, LN1s, wTd, OUTs, mixkey, afunc, TOK, odt=BF16, filler=None,
              biasv=None):
    cfg, S, blk, sb, ps = G["cfg"], G["S"], G["blk"], G["sb"], G["ps"]
    _ph = contextlib.ExitStack()
    sb = phase_sb(G, _ph)
    barrier(G)
    C, CCH, F = cfg.C, cfg.CCH, cfg.F_KV
    NT = TOK // F
    goff = cfg.TT - TOK
    FP = F + 1
    mixv = G["mix"][mixkey]
    wsb = sb(f"{tag}_w", [128, CCH * C], BF16)
    lnb = [sb(f"{tag}_ln{i}", [128, CCH * FP], BF16) for i in range(2)]
    xm = [sb(f"{tag}_xm{i}", [128, CCH * F], BF16) for i in range(2)]
    ob = [sb(f"{tag}_o{i}", [128, CCH * F], odt) for i in range(2)]
    ev_w, ev_ln, ev_lerp, ev_store = {}, {}, {}, {}
    ev_mm_last, ev_evac = {}, {}

    @blk.gpsimd
    def _(g):
        ev_w[0] = S.dma_seq(g, [(wsb[:, ch * C:(ch + 1) * C], wTd[ch])
                                for ch in range(CCH)], "dgp")

    ts0 = [None]
    for j in range(NT):
        @blk.sync
        def _(sy, j=j):
            S.wait(sy, G["ev_p0_done"])
            if j >= 2:
                S.wait(sy, ev_lerp[j - 2])
            g0 = goff + j * F
            if g0 == 0:
                ev_ln[j] = S.dma(sy, 
                    out=pcf(lnb[j % 2][:], FP)[:, :, 1:FP],
                    in_=dpcf(LN1s[:, :, 0:F]), pool="din")
            else:
                ev_ln[j] = S.dma(sy, 
                    out=pcf(lnb[j % 2][:], FP),
                    in_=dpcf(LN1s[:, :, g0 - 1:g0 + F]), pool="din")

        @blk.vector
        def _(v, j=j):
            S.wait(v, ev_ln[j])
            if j == 0 and goff == 0:
                em = S.fire(v.memset(pcf(lnb[0][:], FP)[:, :, 0:1], 0.0), "tv")
                S.wait(v, em)
            if j >= 2:
                S.wait(v, ev_mm_last[j - 2])
            for ch in range(CCH):
                cur = lnb[j % 2][:, ch * FP + 1:ch * FP + 1 + F]
                sh = lnb[j % 2][:, ch * FP:ch * FP + F]
                ins = v.tensor_tensor(xm[j % 2][:, ch * F:(ch + 1) * F], cur, sh,
                                      ALU.subtract)
            e1 = S.fire(ins, "tv")
            S.wait(v, e1)
            for ch in range(CCH):
                sh = lnb[j % 2][:, ch * FP:ch * FP + F]
                ins = v.scalar_tensor_tensor(
                    xm[j % 2][:, ch * F:(ch + 1) * F],
                    xm[j % 2][:, ch * F:(ch + 1) * F],
                    mixv[:, ch:ch + 1], sh, ALU.mult, ALU.add)
            ev_lerp[j] = S.fire(ins, "tv")

        @blk.tensor
        def _(t, j=j):
            if j == 0:
                S.wait(t, ev_w[0])
                ts0[0] = S.c["tse"]
            S.wait(t, ev_lerp[j])
            for dch in range(CCH):
                grp = j * CCH + dch
                nbk = min(4, CCH)
                t.wait_ge(S.s["tse"], ts0[0] + max(0, grp - (nbk - 1)))
                for ch in range(CCH):
                    ins = t.matmul(ps[dch % 4][:, 0:F],
                                   wsb[:, ch * C + dch * 128:ch * C + (dch + 1) * 128],
                                   xm[j % 2][:, ch * F:(ch + 1) * F],
                                   start=(ch == 0), stop=(ch == CCH - 1))
                ev_evac[(j, dch)] = S.fire(ins, "tt")
            ev_mm_last[j] = ev_evac[(j, CCH - 1)]

        @blk.scalar
        def _(sc, j=j, afunc=afunc):
            for dch in range(CCH):
                S.wait(sc, ev_evac[(j, dch)])
                if dch == 0 and j >= 2:
                    S.wait(sc, ev_store[j - 2])
                kw = {} if biasv is None else {"bias": biasv[:, dch:dch + 1]}
                ins = sc.activation(ob[j % 2][:, dch * F:(dch + 1) * F],
                                    ps[dch % 4][:, 0:F], afunc, **kw)
                S.fire(ins, "tse")
            S.wait(sc, ("tse", S.c["tse"]))
            ev_store[j] = S.dma(sc,
                out=dpcf(OUTs[:, :, j * F:(j + 1) * F]),
                in_=pcf(ob[j % 2][:], F), pool="dsc")

        if filler is not None:
            filler()
    G[f"ev_{tag}_done"] = S.pool_snap("dsc")
    _ph.close()


def emit_phase_v(G, LN1s, wvTd, As, NXs, DXs):
    cfg, S, blk, sb, ps = G["cfg"], G["S"], G["blk"], G["sb"], G["ps"]
    _ph = contextlib.ExitStack()
    sb = phase_sb(G, _ph)
    barrier(G)
    C, CCH, F, TT, HALO = cfg.C, cfg.CCH, cfg.F_KV, cfg.TT, cfg.HALO
    NT = TT // F
    jb = HALO // F
    FP = F + 1
    lam, flag, mixv = G["lam"], G["flag"], G["mix"]["v"]
    wsb = sb("V_w", [128, CCH * C], BF16)
    lnb = [sb("V_ln0", [128, CCH * FP], BF16)]
    xm = [sb(f"V_xm{i}", [128, CCH * F], BF16) for i in range(2)]
    vb = sb("V_v", [128, CCH * F])
    pv = sb("V_pv", [128, CCH * F])
    ab = [sb("V_a0", [128, CCH * F], F32)]
    nsc = sb("V_nsc", [128, CCH * FP])
    dsc_ = sb("V_dsc", [128, CCH * FP])
    ev_ln, ev_a, ev_lerp, ev_mm_last, ev_evac = {}, {}, {}, {}, {}
    ev_pv, ev_scan, ev_nst, ev_dst = {}, {}, {}, {}

    @blk.gpsimd
    def _(g):
        G["_V_w"] = S.dma_seq(g, [(wsb[:, ch * C:(ch + 1) * C], wvTd[ch])
                                  for ch in range(CCH)], "dgp")

    ts0 = [None]
    for j in range(NT):
        @blk.sync
        def _(sy, j=j):
            S.wait(sy, G["ev_K_done"])
            if j >= 1:
                S.wait(sy, ev_lerp[j - 1])
            g0 = j * F
            if g0 == 0:
                ev_ln[j] = S.dma(sy, 
                    out=pcf(lnb[0][:], FP)[:, :, 1:FP],
                    in_=dpcf(LN1s[:, :, 0:F]), pool="din")
            else:
                ev_ln[j] = S.dma(sy, 
                    out=pcf(lnb[0][:], FP),
                    in_=dpcf(LN1s[:, :, g0 - 1:g0 + F]), pool="din")
            if j >= 1:
                S.wait(sy, ev_scan[j - 1])
            ev_a[j] = S.dma(sy,
                out=pcf(ab[0][:], F),
                in_=dpcf(As[:, :, j * F:(j + 1) * F]), pool="din")

        @blk.vector
        def _(v, j=j):
            S.wait(v, ev_ln[j])
            if j == 0:
                v.memset(pcf(lnb[0][:], FP)[:, :, 0:1], 0.0)
                v.memset(pcf(nsc[:], FP)[:, :, 0:1], 0.0)
                em = S.fire(v.memset(pcf(dsc_[:], FP)[:, :, 0:1], 0.0), "tv")
                S.wait(v, em)
            if j >= 2:
                S.wait(v, ev_mm_last[j - 2])
            for ch in range(CCH):
                cur = lnb[0][:, ch * FP + 1:ch * FP + 1 + F]
                sh = lnb[0][:, ch * FP:ch * FP + F]
                ins = v.tensor_tensor(xm[j % 2][:, ch * F:(ch + 1) * F], cur, sh,
                                      ALU.subtract)
            e1 = S.fire(ins, "tv")
            S.wait(v, e1)
            for ch in range(CCH):
                sh = lnb[0][:, ch * FP:ch * FP + F]
                ins = v.scalar_tensor_tensor(
                    xm[j % 2][:, ch * F:(ch + 1) * F],
                    xm[j % 2][:, ch * F:(ch + 1) * F],
                    mixv[:, ch:ch + 1], sh, ALU.mult, ALU.add)
            ev_lerp[j] = S.fire(ins, "tv")

        @blk.tensor
        def _(t, j=j):
            if j == 0:
                S.wait(t, G["_V_w"])
                ts0[0] = S.c["tse"]
            S.wait(t, ev_lerp[j])
            for dch in range(CCH):
                grp = j * CCH + dch
                nbk = min(4, CCH)
                t.wait_ge(S.s["tse"], ts0[0] + max(0, grp - (nbk - 1)))
                for ch in range(CCH):
                    ins = t.matmul(ps[dch % 4][:, 0:F],
                                   wsb[:, ch * C + dch * 128:ch * C + (dch + 1) * 128],
                                   xm[j % 2][:, ch * F:(ch + 1) * F],
                                   start=(ch == 0), stop=(ch == CCH - 1))
                ev_evac[(j, dch)] = S.fire(ins, "tt")
            ev_mm_last[j] = ev_evac[(j, CCH - 1)]

        @blk.scalar
        def _(sc, j=j):
            for dch in range(CCH):
                S.wait(sc, ev_evac[(j, dch)])
                if dch == 0 and j >= 1:
                    S.wait(sc, ev_pv[j - 1])
                S.fire(sc.copy(vb[:, dch * F:(dch + 1) * F], ps[dch % 4][:, 0:F]),
                       "tse")

        @blk.vector
        def _(v, j=j):
            v.wait_ge(S.s["tse"], ts0[0] + (j + 1) * CCH)
            S.wait(v, ev_a[j])
            if j >= 1:
                S.wait(v, ev_scan[j - 1])
            for ch in range(CCH):
                ins = v.tensor_tensor(pv[:, ch * F:(ch + 1) * F],
                                      ab[0][:, ch * F:(ch + 1) * F],
                                      vb[:, ch * F:(ch + 1) * F], ALU.mult)
            ev_pv[j] = S.fire(ins, "tv")
            S.wait(v, ev_pv[j])
            if j > 0:
                if j - 1 >= jb:
                    S.wait(v, ev_nst[j - 1])
                    S.wait(v, ev_dst[j - 1])
                ncf = pcf(nsc[:], FP)
                dcf = pcf(dsc_[:], FP)
                if j == jb:
                    v.tensor_scalar(ncf[:, :, 0:1], ncf[:, :, FP - 1:FP],
                                    flag[:, 0:1], None, ALU.mult)
                    ins = v.tensor_scalar(dcf[:, :, 0:1], dcf[:, :, FP - 1:FP],
                                          flag[:, 0:1], None, ALU.mult)
                else:
                    v.tensor_copy(ncf[:, :, 0:1], ncf[:, :, FP - 1:FP])
                    ins = v.tensor_copy(dcf[:, :, 0:1], dcf[:, :, FP - 1:FP])
                e2 = S.fire(ins, "tv")
                S.wait(v, e2)
            for ch in range(CCH):
                lam_b = lam[:, ch:ch + 1].broadcast_to((128, F))
                v.tensor_tensor_scan(nsc[:, ch * FP + 1:ch * FP + 1 + F], lam_b,
                                     pv[:, ch * F:(ch + 1) * F],
                                     nsc[:, ch * FP:ch * FP + 1],
                                     ALU.mult, ALU.add)
                ins = v.tensor_tensor_scan(dsc_[:, ch * FP + 1:ch * FP + 1 + F],
                                           lam_b, ab[0][:, ch * F:(ch + 1) * F],
                                           dsc_[:, ch * FP:ch * FP + 1],
                                           ALU.mult, ALU.add)
            ev_scan[j] = S.fire(ins, "tv")

        if j >= jb:
            @blk.scalar
            def _(sc, j=j):
                S.wait(sc, ev_scan[j])
                t1 = j * F - HALO
                w = FP if j == NT - 1 else F
                ev_nst[j] = S.dma(sc, 
                    out=dpcf(NXs[:, :, t1:t1 + w]),
                    in_=pcf(nsc[:], FP)[:, :, 0:w], pool="dsc")
                ev_dst[j] = S.dma(sc, 
                    out=dpcf(DXs[:, :, t1:t1 + w]),
                    in_=pcf(dsc_[:], FP)[:, :, 0:w], pool="dsc")
        else:
            ev_nst[j] = ev_dst[j] = None
    G["ev_V_done"] = S.pool_snap("dsc")
    _ph.close()


def emit_phase_o(G, NXs, DXs, SRs, woTd, AOLNs, filler_factory=None):
    cfg, S, blk, sb, ps = G["cfg"], G["S"], G["blk"], G["sb"], G["ps"]
    _ph = contextlib.ExitStack()
    sb = phase_sb(G, _ph)
    barrier(G)
    filler = filler_factory(sb) if filler_factory is not None else None
    C, CCH, F, TOWN = cfg.C, cfg.CCH, cfg.F_O, cfg.TOWN
    NT = TOWN // F
    FP = F + 1
    k1c = G["k1c"]
    FB = 128
    NSB = F // FB
    FPB = FB + 1
    wsb = sb("O_w", [128, CCH * C], BF16)
    nxw = [sb(f"O_nx{i}", [128, CCH * FPB]) for i in range(2)]
    dxw = [sb(f"O_dx{i}", [128, CCH * FPB]) for i in range(2)]
    srb = [sb(f"O_sr{i}", [128, CCH * F], BF16) for i in range(2)]
    bt1 = sb("O_t1", [128, CCH * FB])
    bt2 = sb("O_t2", [128, CCH * FB])
    rwk = [sb(f"O_rw{i}", [128, CCH * F], BF16) for i in range(2)]
    sq = [sb("O_sq0", [128, F], BF16)]
    rows = {k: sb(f"O_r{k}", [1, F]) for k in ("m", "r", "t1", "t2")}
    mb = sb("O_mb", [128, F])
    rb = sb("O_rb", [128, F])
    aol = [sb(f"O_aol{i}", [128, CCH * F], BF16) for i in range(2)]
    ev_in, ev_blend, ev_mm_last, ev_evac, ev_win, ev_bsub = {}, {}, {}, {}, {}, {}
    ev_aol, ev_store, ev_mbrb, ev_aodone, ev_sqdone = {}, {}, {}, {}, {}

    @blk.gpsimd
    def _(g):
        G["_O_w"] = S.dma_seq(g, [(wsb[:, ch * C:(ch + 1) * C], woTd[ch])
                                  for ch in range(CCH)], "dgp")

    ts0 = [None]
    post = {}
    for j in range(NT):
        @blk.sync
        def _(sy, j=j):
            S.wait(sy, G["ev_V_done"])
            S.wait(sy, G["ev_R_done"])
            if j >= 2:
                S.wait(sy, ev_blend[j - 2])
            ev_in[j] = S.dma(sy, out=pcf(srb[j % 2][:], F),
                             in_=dpcf(SRs[:, :, j * F:(j + 1) * F]), pool="din")
            for sbt in range(NSB):
                k = j * NSB + sbt
                if k >= 2:
                    S.wait(sy, ev_bsub[k - 2])
                g0 = j * F + sbt * FB
                ev_win[k] = S.dma_seq(sy, [
                    (pcf(nxw[k % 2][:], FPB), dpcf(NXs[:, :, g0:g0 + FPB])),
                    (pcf(dxw[k % 2][:], FPB), dpcf(DXs[:, :, g0:g0 + FPB]))],
                    "din")

        for sbt in range(NSB):
            k = j * NSB + sbt
            o = sbt * FB

            @blk.vector
            def _(v, j=j, sbt=sbt, k=k, o=o):
                S.wait(v, ev_win[k])
                S.wait(v, ev_in[j])
                if k >= 1:
                    S.wait(v, ev_bsub.get(k - 1, None))
                if j >= 2:
                    S.wait(v, ev_mm_last[j - 2])
                for ch in range(CCH):
                    v.scalar_tensor_tensor(bt1[:, ch * FB:(ch + 1) * FB],
                                           nxw[k % 2][:, ch * FPB:ch * FPB + FB],
                                           k1c[:, ch:ch + 1],
                                           nxw[k % 2][:, ch * FPB + 1:ch * FPB + 1 + FB],
                                           ALU.mult, ALU.add)
                    ins = v.scalar_tensor_tensor(bt2[:, ch * FB:(ch + 1) * FB],
                                                 dxw[k % 2][:, ch * FPB:ch * FPB + FB],
                                                 k1c[:, ch:ch + 1],
                                                 dxw[k % 2][:, ch * FPB + 1:ch * FPB + 1 + FB],
                                                 ALU.mult, ALU.add)
                e2 = S.fire(ins, "tv")
                S.wait(v, e2)
                e3 = S.fire(v.reciprocal(bt2[:], bt2[:]), "tv")
                S.wait(v, e3)
                e4 = S.fire(v.tensor_tensor(bt1[:], bt1[:], bt2[:], ALU.mult),
                            "tv")
                S.wait(v, e4)
                for ch in range(CCH):
                    ins = v.tensor_tensor(
                        rwk[j % 2][:, ch * F + o:ch * F + o + FB],
                        bt1[:, ch * FB:(ch + 1) * FB],
                        srb[j % 2][:, ch * F + o:ch * F + o + FB], ALU.mult)
                ev_bsub[k] = S.fire(ins, "tv")
        ev_blend[j] = ev_bsub[j * NSB + NSB - 1]

        @blk.tensor
        def _(t, j=j):
            if j == 0:
                S.wait(t, G["_O_w"])
                ts0[0] = S.c["tse"]
            S.wait(t, ev_blend[j])
            for dch in range(CCH):
                grp = j * CCH + dch
                nbk = min(4, CCH)
                t.wait_ge(S.s["tse"], ts0[0] + max(0, grp - (nbk - 1)))
                for ch in range(CCH):
                    ins = t.matmul(ps[dch % 4][:, 0:F],
                                   wsb[:, ch * C + dch * 128:ch * C + (dch + 1) * 128],
                                   rwk[j % 2][:, ch * F:(ch + 1) * F],
                                   start=(ch == 0), stop=(ch == CCH - 1))
                ev_evac[(j, dch)] = S.fire(ins, "tt")
            ev_mm_last[j] = ev_evac[(j, CCH - 1)]

        @blk.scalar
        def _(sc, j=j):
            for dch in range(CCH):
                S.wait(sc, ev_evac[(j, dch)])
                if dch == 0 and j >= 2:
                    S.wait(sc, ev_store[j - 2])
                S.fire(sc.copy(aol[j % 2][:, dch * F:(dch + 1) * F],
                               ps[dch % 4][:, 0:F]), "tse")
        ev_aoev = ("tse", S.c["tse"])

        def make_post(j, ev_aoev):
            def run():
                srcs = [aol[j % 2][:, ch * F:(ch + 1) * F]
                        for ch in range(CCH)]
                ev_sum, ev_sq = emit_sums_sq(G, F, srcs, sq, G["ones_cb"],
                                             4, 5, ev_aoev,
                                             ev_sqdone.get(j - 1),
                                             ev_mbrb.get(j - 1))
                ev_sqdone[j] = ev_sq
                ev_psf, ev_bc = ln_rows(G, F, (ev_sum, ev_sq), 4, 5, 6, 7,
                                        rows, C, ev_psfree=ev_mbrb.get(j - 1))

                @blk.vector
                def _(v, j=j, ev_bc=ev_bc):
                    S.wait(v, ev_bc)
                    v.tensor_copy(mb[:], ps[6][:, 0:F])
                    ev_mbrb[j] = S.fire(v.tensor_copy(rb[:], ps[7][:, 0:F]),
                                        "tv")
                    S.wait(v, ev_mbrb[j])
                    for ch in range(CCH):
                        ins = v.tensor_tensor(aol[j % 2][:, ch * F:(ch + 1) * F],
                                              aol[j % 2][:, ch * F:(ch + 1) * F],
                                              mb[:], ALU.subtract)
                    e1 = S.fire(ins, "tv")
                    S.wait(v, e1)
                    for ch in range(CCH):
                        ins = v.tensor_tensor(aol[j % 2][:, ch * F:(ch + 1) * F],
                                              aol[j % 2][:, ch * F:(ch + 1) * F],
                                              rb[:], ALU.mult)
                    ev_aol[j] = S.fire(ins, "tv")
                    ev_aodone[j] = ev_aol[j]

                @blk.scalar
                def _(sc, j=j):
                    S.wait(sc, ev_aol[j])
                    ev_store[j] = S.dma(sc,
                        out=dpcf(AOLNs[:, :, j * F:(j + 1) * F]),
                        in_=pcf(aol[j % 2][:], F), pool="dsc")
            return run

        if j >= 1:
            post[j - 1]()
        post[j] = make_post(j, ev_aoev)

        if filler is not None:
            filler()
    post[NT - 1]()
    G["ev_O_done"] = S.pool_snap("dsc")
    _ph.close()


def emit_ffnpre(G, xT, AOLNs, XQs, X2s):
    cfg, S, blk, sb, ps = G["cfg"], G["S"], G["blk"], G["sb"], G["ps"]
    _ph = contextlib.ExitStack()
    sb = phase_sb(G, _ph)
    barrier(G)
    C, CCH, TOWN, HALO = cfg.C, cfg.CCH, cfg.TOWN, cfg.HALO
    F = cfg.F_O
    NT = TOWN // F
    NB = F // 128
    wqs, ones_cf, ones_rf, identf = (G["wqs"], G["ones_cf"], G["ones_rf"],
                                     G["identf"])
    u2g_b, u2u_b = G["u2g_b"], G["u2u_b"]
    xf = [sb(f"FP_x{i}", [128, CCH * F]) for i in range(2)]
    aolb = [sb(f"FP_ao{i}", [128, CCH * F], BF16) for i in range(2)]
    x2 = sb("FP_x2", [128, CCH * F])
    zb = sb("FP_zb", [128, CCH * F])
    sq = [sb(f"FP_sq{i}", [128, F], BF16) for i in range(2)]
    shad = sb("FP_shad", [128, CCH * F], BF16)
    fmax = sb("FP_fmax", [128, F])
    fmin = sb("FP_fmin", [128, F])
    tmcol = sb("FP_tmcol", [128, 2])
    rows = {k: sb(f"FP_r{k}", [1, F]) for k in
            ("m", "r", "t1", "t2", "g", "mm", "cs", "mcs", "ug", "uu")}
    csb = sb("FP_csb", [128, F])
    mcsb = sb("FP_mcsb", [128, F])
    xq = [sb(f"FP_xq{i}", [128, CCH * F], BF16) for i in range(2)]
    ev_in, ev_x2, ev_fold, ev_q, ev_store = {}, {}, {}, {}, {}
    ev_psmax, ev_rowsdone, ev_bcread = {}, {}, {}
    ev_x2free, ev_bcfinal, ev_sqdone, ev_x2st = {}, {}, {}, {}
    tm_ev = {}

    for j in range(NT):
        @blk.sync
        def _(sy, j=j):
            S.wait(sy, G["ev_O_done"])
            if j >= 2:
                S.wait(sy, ev_q[j - 2])
            ev_in[j] = S.dma_seq(sy, [
                (pcf(xf[j % 2][:], F),
                 dpcf(xT[:, :, HALO + j * F:HALO + (j + 1) * F])),
                (pcf(aolb[j % 2][:], F),
                 dpcf(AOLNs[:, :, j * F:(j + 1) * F]))], "din")

        @blk.vector
        def _(v, j=j):
            S.wait(v, ev_in[j])
            if j >= 1:
                S.wait(v, ev_store[j - 1])
                S.wait(v, ev_x2st[j - 1])
            ev_x2[j] = S.fire(v.tensor_tensor(x2[:], xf[j % 2][:],
                                              aolb[j % 2][:], ALU.add), "tv")
            S.wait(v, ev_x2[j])
            if j >= 1:
                S.wait(v, tm_ev[(j - 1, NB - 1, "tr")])
            v.tensor_copy(fmax[:], x2[:, 0:F])
            e1 = S.fire(v.tensor_copy(fmin[:], x2[:, 0:F]), "tv")
            S.wait(v, e1)
            for ch in range(1, CCH):
                v.tensor_tensor(fmax[:], fmax[:], x2[:, ch * F:(ch + 1) * F],
                                ALU.max)
                ins = v.tensor_tensor(fmin[:], fmin[:],
                                      x2[:, ch * F:(ch + 1) * F], ALU.min)
            ev_fold[j] = S.fire(ins, "tv") if CCH > 1 else e1

        srcs = [x2[:, ch * F:(ch + 1) * F] for ch in range(CCH)]
        ev_sum, ev_sq = emit_sums_sq(G, F, srcs, sq, G["ones_cb"], 4, 5,
                                     ev_x2[j], ev_sqdone.get(j - 1),
                                     ev_bcfinal.get(j - 1), shadow=shad)
        ev_sqdone[j] = ev_sq

        for b in range(NB):
            @blk.tensor
            def _(t, j=j, b=b):
                S.wait(t, ev_fold[j])
                if b == 0:
                    if j >= 1:
                        S.wait(t, ev_bcfinal[j - 1])
                else:
                    S.wait(t, tm_ev[(j, b - 1, "red")])
                t.transpose(ps[6][:, 0:128], fmax[:, b * 128:(b + 1) * 128],
                            identf[:])
                tm_ev[(j, b, "tr")] = S.fire(
                    t.transpose(ps[6][:, 128:256], fmin[:, b * 128:(b + 1) * 128],
                                identf[:]), "tt")

            @blk.vector
            def _(v, j=j, b=b):
                S.wait(v, tm_ev[(j, b, "tr")])
                if b > 0:
                    S.wait(v, tm_ev[(j, b - 1, "row")])
                elif j >= 1:
                    S.wait(v, tm_ev[(j - 1, NB - 1, "row")])
                v.tensor_reduce(tmcol[:, 0:1], ps[6][:, 0:128], AX.X, ALU.max)
                tm_ev[(j, b, "red")] = S.fire(
                    v.tensor_reduce(tmcol[:, 1:2], ps[6][:, 128:256], AX.X,
                                    ALU.min), "tv")

            @blk.tensor
            def _(t, j=j, b=b):
                S.wait(t, tm_ev[(j, b, "red")])
                t.transpose(ps[7][0:1, b * 128:(b + 1) * 128],
                            tmcol[:, 0:1], identf[:])
                tm_ev[(j, b, "row")] = S.fire(
                    t.transpose(ps[7][0:1, F + b * 128:F + (b + 1) * 128],
                                tmcol[:, 1:2], identf[:]), "tt")
        ev_psmax[j] = tm_ev[(j, NB - 1, "row")]

        @blk.vector
        def _(v, j=j, ev_sum=ev_sum, ev_sq=ev_sq):
            S.waits(v, [ev_sum, ev_sq, ev_psmax[j]])
            v.tensor_scalar(rows["m"][:], ps[4][0:1, 0:F], 1.0 / C, None,
                            ALU.mult)
            v.tensor_scalar(rows["t1"][:], ps[5][0:1, 0:F], 1.0 / C, None,
                            ALU.mult)
            v.tensor_copy(rows["t2"][:], ps[7][0:1, 0:F])
            e0 = S.fire(v.tensor_copy(rows["mm"][:], ps[7][0:1, F:2 * F]), "tv")
            ev_bcread[j] = e0
            S.wait(v, e0)
            e0b = S.fire(v.tensor_tensor(rows["r"][:], rows["m"][:],
                                         rows["m"][:], ALU.mult), "tv")
            S.wait(v, e0b)
            e1 = S.fire(v.tensor_tensor(rows["t1"][:], rows["t1"][:],
                                        rows["r"][:], ALU.subtract), "tv")
            S.wait(v, e1)
            v.tensor_tensor(rows["t2"][:], rows["t2"][:], rows["m"][:],
                            ALU.subtract)
            e2 = S.fire(v.tensor_tensor(rows["mm"][:], rows["m"][:],
                                        rows["mm"][:], ALU.subtract), "tv")
            S.wait(v, e2)
            e3 = S.fire(v.tensor_tensor(rows["mm"][:], rows["mm"][:],
                                        rows["t2"][:], ALU.max), "tv")
            S.wait(v, e3)
            eC = S.fire(v.tensor_copy(rows["g"][:], rows["t1"][:]), "tv")
            S.wait(v, eC)
            G["_fp_am"] = S.fire(v.tensor_scalar(rows["t1"][:], rows["t1"][:],
                                                 LN_EPS, None, ALU.add), "tv")

        @blk.scalar
        def _(sc, j=j):
            S.wait(sc, G["_fp_am"])
            e = S.fire(sc.activation(rows["t2"][:], rows["t1"][:], AF.Ln), "ts")
            S.wait(sc, e)
            G["_fp_sd"] = S.fire(sc.activation(rows["r"][:], rows["t2"][:],
                                               AF.Exp, scale=-0.5), "ts")

        @blk.vector
        def _(v, j=j):
            S.wait(v, G["_fp_sd"])
            e1b = S.fire(v.tensor_tensor(rows["t2"][:], rows["r"][:],
                                         rows["r"][:], ALU.mult), "tv")
            S.wait(v, e1b)
            e2 = S.fire(v.tensor_tensor(rows["t2"][:], rows["t2"][:],
                                        rows["t1"][:], ALU.mult), "tv")
            S.wait(v, e2)
            e3 = S.fire(v.tensor_scalar(rows["t2"][:], rows["t2"][:], -0.5, 1.5,
                                        ALU.mult, ALU.add), "tv")
            S.wait(v, e3)
            e4 = S.fire(v.tensor_tensor(rows["r"][:], rows["r"][:],
                                        rows["t2"][:], ALU.mult), "tv")
            S.wait(v, e4)
            v.tensor_tensor(rows["t2"][:], rows["r"][:], rows["r"][:], ALU.mult)
            e5 = S.fire(v.engine_nop(), "tv")
            S.wait(v, e5)
            e7 = S.fire(v.tensor_tensor(rows["t2"][:], rows["t2"][:],
                                        rows["g"][:], ALU.mult), "tv")
            S.wait(v, e7)
            G["_fp_u"] = S.fire(v.tensor_scalar(rows["t2"][:], rows["t2"][:],
                                                RMS_EPS, None, ALU.add), "tv")

        @blk.scalar
        def _(sc, j=j):
            S.wait(sc, G["_fp_u"])
            e = S.fire(sc.activation(rows["t1"][:], rows["t2"][:], AF.Ln), "ts")
            S.wait(sc, e)
            G["_fp_sg"] = S.fire(sc.activation(rows["g"][:], rows["t1"][:],
                                               AF.Exp, scale=-0.5), "ts")

        @blk.vector
        def _(v, j=j):
            S.wait(v, G["_fp_sg"])
            e2 = S.fire(v.tensor_tensor(rows["t1"][:], rows["g"][:],
                                        rows["g"][:], ALU.mult), "tv")
            S.wait(v, e2)
            e3 = S.fire(v.tensor_tensor(rows["t1"][:], rows["t1"][:],
                                        rows["t2"][:], ALU.mult), "tv")
            S.wait(v, e3)
            e4 = S.fire(v.tensor_scalar(rows["t1"][:], rows["t1"][:], -0.5, 1.5,
                                        ALU.mult, ALU.add), "tv")
            S.wait(v, e4)
            e5 = S.fire(v.tensor_tensor(rows["g"][:], rows["g"][:],
                                        rows["t1"][:], ALU.mult), "tv")
            S.wait(v, e5)
            e6 = S.fire(v.tensor_tensor(rows["g"][:], rows["g"][:],
                                        rows["r"][:], ALU.mult), "tv")
            S.wait(v, e6)
            e7 = S.fire(v.tensor_tensor(rows["t1"][:], rows["mm"][:],
                                        rows["g"][:], ALU.mult), "tv")
            S.wait(v, e7)
            e8 = S.fire(v.tensor_scalar(rows["t1"][:], rows["t1"][:], Q_EPS,
                                        None, ALU.max), "tv")
            S.wait(v, e8)
            e9 = S.fire(v.reciprocal(rows["t2"][:], rows["t1"][:]), "tv")
            S.wait(v, e9)
            e10 = S.fire(v.tensor_scalar(rows["t2"][:], rows["t2"][:], 127.0,
                                         None, ALU.mult), "tv")
            S.wait(v, e10)
            e11 = S.fire(v.tensor_tensor(rows["cs"][:], rows["g"][:],
                                         rows["t2"][:], ALU.mult), "tv")
            S.wait(v, e11)
            e12 = S.fire(v.tensor_tensor(rows["mcs"][:], rows["m"][:],
                                         rows["cs"][:], ALU.mult), "tv")
            S.wait(v, e12)
            e13 = S.fire(v.tensor_scalar(rows["t1"][:], rows["t1"][:],
                                         1.0 / 127.0, None, ALU.mult), "tv")
            S.wait(v, e13)
            v.tensor_scalar(rows["ug"][:], rows["t1"][:], wqs[0:1, 3:4], None,
                            ALU.mult)
            G["_fp_rows"] = S.fire(v.tensor_scalar(rows["uu"][:], rows["t1"][:],
                                                   wqs[0:1, 4:5], None,
                                                   ALU.mult), "tv")
        ev_rowsdone[j] = G["_fp_rows"]

        @blk.tensor
        def _(t, j=j):
            S.wait(t, G["_fp_rows"])
            t.matmul(ps[6][:, 0:F], ones_rf[:], rows["cs"][0:1, :], start=True,
                     stop=True)
            t.matmul(ps[7][:, 0:F], ones_rf[:], rows["mcs"][0:1, :], start=True,
                     stop=True)
            t.matmul(ps[4][:, 0:F], ones_rf[:], rows["ug"][0:1, :], start=True,
                     stop=True)
            G["_fp_bc"] = S.fire(t.matmul(ps[5][:, 0:F], ones_rf[:],
                                          rows["uu"][0:1, :], start=True,
                                          stop=True), "tt")

        @blk.vector
        def _(v, j=j):
            S.wait(v, G["_fp_bc"])
            v.tensor_copy(csb[:], ps[6][:, 0:F])
            v.tensor_copy(mcsb[:], ps[7][:, 0:F])
            v.tensor_copy(u2g_b[:, j * F:(j + 1) * F], ps[4][:, 0:F])
            e0 = S.fire(v.tensor_copy(u2u_b[:, j * F:(j + 1) * F],
                                      ps[5][:, 0:F]), "tv")
            ev_bcfinal[j] = e0
            S.wait(v, e0)
            for ch in range(CCH):
                ins = v.tensor_tensor(zb[:, ch * F:(ch + 1) * F],
                                      x2[:, ch * F:(ch + 1) * F], csb[:],
                                      ALU.mult)
            e1 = S.fire(ins, "tv")
            S.wait(v, e1)
            for ch in range(CCH):
                ins = v.tensor_tensor(zb[:, ch * F:(ch + 1) * F],
                                      zb[:, ch * F:(ch + 1) * F], mcsb[:],
                                      ALU.subtract)
            e2 = S.fire(ins, "tv")
            S.wait(v, e2)
            if j >= 2:
                S.wait(v, ev_store[j - 2])
            ev_q[j] = S.fire(v.tensor_scalar(xq[j % 2][:], zb[:], MAGIC, -MAGIC,
                                             ALU.add, ALU.add), "tv")

        @blk.scalar
        def _(sc, j=j):
            S.wait(sc, ev_q[j])
            ev_x2st[j] = S.dma(sc, out=dpcf(X2s[:, :, j * F:(j + 1) * F]),
                               in_=pcf(x2[:], F), pool="dsc")
            ev_store[j] = S.dma(sc,
                out=dpcf(XQs[:, :, j * F:(j + 1) * F]),
                in_=pcf(xq[j % 2][:], F), pool="dsc")
    G["ev_FP_done"] = S.pool_snap("dsc")
    _ph.close()


def emit_gu(G, XQs, W1Q, W2Q, Hs):
    cfg, S, blk, sb, ps = G["cfg"], G["S"], G["blk"], G["sb"], G["ps"]
    _ph = contextlib.ExitStack()
    sb = phase_sb(G, _ph)
    barrier(G)
    C, H, CCH, HCH, TOWN = cfg.C, cfg.H, cfg.CCH, cfg.HCH, cfg.TOWN
    TH = cfg.THALF
    NHALF = TOWN // TH
    FQ = min(512, TH)
    NQ = TH // FQ
    wqs, ones_rf, identf = G["wqs"], G["ones_rf"], G["identf"]
    u2g_b, u2u_b = G["u2g_b"], G["u2u_b"]
    xqh = sb("GU_xq", [128, CCH * TH], BF16)
    w1b = [sb(f"GU_w1{i}", [128, CCH * 128], BF16) for i in range(2)]
    w2b = [sb(f"GU_w2{i}", [128, CCH * 128], BF16) for i in range(2)]
    gb = sb("GU_g", [128, TH])
    ub = sb("GU_u", [128, TH])
    hb = [sb(f"GU_h{i}", [128, TH], BF16) for i in range(2)]
    sqa = sb("GU_sqa", [128, TOWN])
    mxa = sb("GU_mxa", [128, TOWN])
    mna = sb("GU_mna", [128, TOWN])
    sqt = sb("GU_sqt", [128, TH])
    cs3_b, u3_b = G["cs3_b"], G["u3_b"]
    ev_xq, ev_w, ev_mm, ev_gu, ev_h, ev_hst, ev_stat = {}, {}, {}, {}, {}, {}, {}

    @blk.vector
    def _(v):
        v.memset(sqa[:], 0.0)
        v.memset(mxa[:], 0.0)
        G["_gu_z"] = S.fire(v.memset(mna[:], 0.0), "tv")

    for half in range(NHALF):
        th0 = half * TH

        @blk.sync
        def _(sy, half=half):
            S.wait(sy, G["ev_FP_done"])
            S.wait(sy, G["ev_w12q_done"])
            if half > 0:
                S.wait(sy, ev_mm[(half - 1, HCH - 1)])
            ev_xq[half] = S.dma(sy, 
                out=pcf(xqh[:], TH),
                in_=dpcf(XQs[:, :, th0:th0 + TH]), pool="din")

        for hb_i in range(HCH):
            @blk.sync
            def _(sy, half=half, hb_i=hb_i):
                if hb_i >= 2:
                    S.wait(sy, ev_mm[(half, hb_i - 2)])
                ev_w[(half, hb_i)] = S.dma_seq(sy, [
                    (w1b[hb_i % 2][:], W1Q[hb_i]),
                    (w2b[hb_i % 2][:], W2Q[hb_i])], "din")

            @blk.tensor
            def _(t, half=half, hb_i=hb_i):
                S.wait(t, ev_xq[half])
                S.wait(t, ev_w[(half, hb_i)])
                if hb_i >= 2:
                    S.wait(t, ev_gu[(half, hb_i - 2)])
                elif half > 0:
                    S.wait(t, ev_gu[(half - 1, HCH - 2 + hb_i)])
                par = hb_i % 2
                for gi, wb in ((0, w1b[par]), (1, w2b[par])):
                    for q in range(NQ):
                        bank = ps[par * 4 + gi * NQ + q]
                        for ch in range(CCH):
                            ins = t.matmul(
                                bank[:, 0:FQ],
                                wb[:, ch * 128:(ch + 1) * 128],
                                xqh[:, ch * TH + q * FQ:ch * TH + (q + 1) * FQ],
                                start=(ch == 0), stop=(ch == CCH - 1))
                ev_mm[(half, hb_i)] = S.fire(ins, "tt")

            @blk.vector
            def _(v, half=half, hb_i=hb_i):
                par = hb_i % 2
                S.wait(v, ev_mm[(half, hb_i)])
                if hb_i >= 1:
                    S.wait(v, ev_h[(half, hb_i - 1)])
                elif half >= 1:
                    S.wait(v, ev_h[(half - 1, HCH - 1)])
                for q in range(NQ):
                    v.tensor_tensor(gb[:, q * FQ:(q + 1) * FQ],
                                    ps[par * 4 + q][:, 0:FQ],
                                    u2g_b[:, th0 + q * FQ:th0 + (q + 1) * FQ],
                                    ALU.mult)
                    ins = v.tensor_tensor(
                        ub[:, q * FQ:(q + 1) * FQ],
                        ps[par * 4 + NQ + q][:, 0:FQ],
                        u2u_b[:, th0 + q * FQ:th0 + (q + 1) * FQ], ALU.mult)
                ev_gu[(half, hb_i)] = S.fire(ins, "tv")

            @blk.scalar
            def _(sc, half=half, hb_i=hb_i):
                S.wait(sc, ev_gu[(half, hb_i)])
                G["_gu_silu"] = S.fire(sc.activation(gb[:], gb[:], AF.Silu),
                                       "ts")

            @blk.vector
            def _(v, half=half, hb_i=hb_i):
                S.wait(v, G["_gu_silu"])
                if hb_i >= 2:
                    S.wait(v, ev_hst[(half, hb_i - 2)])
                elif half > 0:
                    S.wait(v, ev_hst[(half - 1, HCH - 2 + hb_i)])
                par = hb_i % 2
                ev_h[(half, hb_i)] = S.fire(
                    v.tensor_tensor(hb[par][:], gb[:], ub[:], ALU.mult), "tv")
                S.wait(v, ev_h[(half, hb_i)])
                v.tensor_tensor(sqt[:], hb[par][:], hb[par][:], ALU.mult)
                v.tensor_tensor(mxa[:, th0:th0 + TH], mxa[:, th0:th0 + TH],
                                hb[par][:], ALU.max)
                e1 = S.fire(v.tensor_tensor(mna[:, th0:th0 + TH],
                                            mna[:, th0:th0 + TH], hb[par][:],
                                            ALU.min), "tv")
                S.wait(v, e1)
                ev_stat[(half, hb_i)] = S.fire(
                    v.tensor_tensor(sqa[:, th0:th0 + TH], sqa[:, th0:th0 + TH],
                                    sqt[:], ALU.add), "tv")

            @blk.scalar
            def _(sc, half=half, hb_i=hb_i):
                S.wait(sc, ev_h[(half, hb_i)])
                ev_hst[(half, hb_i)] = S.dma(
                    sc, out=Hs[hb_i, :, th0:th0 + TH], in_=hb[hb_i % 2][:],
                    pool="dsc")

    NBT = TOWN // 128
    tmcol = sb("GU_tm", [128, 3])
    rows = sb("GU_rows", [1, TOWN])
    rowm2 = sb("GU_rowm2", [1, TOWN])
    rt1 = sb("GU_rt1", [1, TOWN])
    rt2 = sb("GU_rt2", [1, TOWN])
    ev_last_stat = ev_stat[(NHALF - 1, HCH - 1)]
    tm_ev = {}
    for b in range(NBT):
        @blk.tensor
        def _(t, b=b):
            S.wait(t, ev_last_stat)
            if b > 0:
                S.wait(t, tm_ev[(b - 1, "red")])
            t.transpose(ps[4][:, 0:128], sqa[:, b * 128:(b + 1) * 128], identf[:])
            t.transpose(ps[4][:, 128:256], mxa[:, b * 128:(b + 1) * 128],
                        identf[:])
            tm_ev[(b, "tr")] = S.fire(
                t.transpose(ps[4][:, 256:384], mna[:, b * 128:(b + 1) * 128],
                            identf[:]), "tt")

        @blk.vector
        def _(v, b=b):
            S.wait(v, tm_ev[(b, "tr")])
            if b > 0:
                S.wait(v, tm_ev[(b - 1, "rowcp")])
            v.tensor_reduce(tmcol[:, 0:1], ps[4][:, 0:128], AX.X, ALU.add)
            v.tensor_reduce(tmcol[:, 1:2], ps[4][:, 128:256], AX.X, ALU.max)
            tm_ev[(b, "red")] = S.fire(
                v.tensor_reduce(tmcol[:, 2:3], ps[4][:, 256:384], AX.X,
                                ALU.min), "tv")

        @blk.tensor
        def _(t, b=b):
            S.wait(t, tm_ev[(b, "red")])
            t.transpose(ps[5][0:1, 0:128], tmcol[:, 0:1], identf[:])
            t.transpose(ps[5][0:1, 128:256], tmcol[:, 1:2], identf[:])
            tm_ev[(b, "row")] = S.fire(
                t.transpose(ps[5][0:1, 256:384], tmcol[:, 2:3], identf[:]),
                "tt")

        @blk.vector
        def _(v, b=b):
            S.wait(v, tm_ev[(b, "row")])
            v.tensor_copy(rows[0:1, b * 128:(b + 1) * 128], ps[5][0:1, 0:128])
            v.tensor_copy(rowm2[0:1, b * 128:(b + 1) * 128],
                          ps[5][0:1, 128:256])
            tm_ev[(b, "rowcp")] = S.fire(
                v.tensor_copy(rt2[0:1, b * 128:(b + 1) * 128],
                              ps[5][0:1, 256:384]), "tv")

    @blk.vector
    def _(v):
        S.wait(v, tm_ev[(NBT - 1, "rowcp")])
        e00 = S.fire(v.tensor_scalar(rt2[:], rt2[:], -1.0, None, ALU.mult),
                     "tv")
        S.wait(v, e00)
        e01 = S.fire(v.tensor_tensor(rowm2[:], rowm2[:], rt2[:], ALU.max), "tv")
        S.wait(v, e01)
        e0 = S.fire(v.tensor_scalar(rows[:], rows[:], 1.0 / H, None, ALU.mult),
                    "tv")
        S.wait(v, e0)
        G["_g3_pre"] = S.fire(v.tensor_scalar(rt1[:], rows[:], RMS_EPS, None,
                                              ALU.add), "tv")

    @blk.scalar
    def _(sc):
        S.wait(sc, G["_g3_pre"])
        e = S.fire(sc.activation(rt2[:], rt1[:], AF.Ln), "ts")
        S.wait(sc, e)
        G["_g3_sq"] = S.fire(sc.activation(rows[:], rt2[:], AF.Exp,
                                           scale=-0.5), "ts")

    @blk.vector
    def _(v):
        S.wait(v, G["_g3_sq"])
        e2 = S.fire(v.tensor_tensor(rt2[:], rows[:], rows[:], ALU.mult), "tv")
        S.wait(v, e2)
        e3 = S.fire(v.tensor_tensor(rt2[:], rt2[:], rt1[:], ALU.mult), "tv")
        S.wait(v, e3)
        e4 = S.fire(v.tensor_scalar(rt2[:], rt2[:], -0.5, 1.5, ALU.mult,
                                    ALU.add), "tv")
        S.wait(v, e4)
        e5 = S.fire(v.tensor_tensor(rows[:], rows[:], rt2[:], ALU.mult), "tv")
        S.wait(v, e5)
        e6 = S.fire(v.tensor_tensor(rt1[:], rowm2[:], rows[:], ALU.mult), "tv")
        S.wait(v, e6)
        e7 = S.fire(v.tensor_scalar(rt1[:], rt1[:], Q_EPS, None, ALU.max), "tv")
        S.wait(v, e7)
        e8 = S.fire(v.reciprocal(rt2[:], rt1[:]), "tv")
        S.wait(v, e8)
        e9 = S.fire(v.tensor_scalar(rt2[:], rt2[:], 127.0, None, ALU.mult), "tv")
        S.wait(v, e9)
        e10 = S.fire(v.tensor_tensor(rt2[:], rows[:], rt2[:], ALU.mult), "tv")
        S.wait(v, e10)
        e11 = S.fire(v.tensor_scalar(rt1[:], rt1[:], 1.0 / 127.0, None,
                                     ALU.mult), "tv")
        S.wait(v, e11)
        G["_g3_rows"] = S.fire(v.tensor_scalar(rt1[:], rt1[:], wqs[0:1, 5:6],
                                               None, ALU.mult), "tv")

    NR = TOWN // min(512, TOWN)
    FR = min(512, TOWN)
    bc_ev = {}
    for r in range(NR):
        @blk.tensor
        def _(t, r=r):
            S.wait(t, G["_g3_rows"])
            if r > 0:
                S.wait(t, bc_ev[(r - 1, "cp")])
            t.matmul(ps[6][:, 0:FR], ones_rf[:], rt2[0:1, r * FR:(r + 1) * FR],
                     start=True, stop=True)
            bc_ev[(r, "mm")] = S.fire(
                t.matmul(ps[7][:, 0:FR], ones_rf[:],
                         rt1[0:1, r * FR:(r + 1) * FR], start=True, stop=True),
                "tt")

        @blk.vector
        def _(v, r=r):
            S.wait(v, bc_ev[(r, "mm")])
            v.tensor_copy(cs3_b[:, r * FR:(r + 1) * FR], ps[6][:, 0:FR])
            bc_ev[(r, "cp")] = S.fire(
                v.tensor_copy(u3_b[:, r * FR:(r + 1) * FR], ps[7][:, 0:FR]),
                "tv")
    G["ev_GU_done"] = [bc_ev[(NR - 1, "cp")]] + S.pool_snap("dsc")
    G["ev_GU_hst"] = ev_hst
    _ph.close()


def emit_w3(G, Hs, W3Q, X2s, outT, DWNs=None):
    cfg, S, blk, sb, ps = G["cfg"], G["S"], G["blk"], G["sb"], G["ps"]
    _ph = contextlib.ExitStack()
    sb = phase_sb(G, _ph)
    barrier(G)
    C, CCH, HCH, TOWN, HALO = cfg.C, cfg.CCH, cfg.HCH, cfg.TOWN, cfg.HALO
    F = cfg.F_W3
    NT = TOWN // F
    cs3_b, u3_b = G["cs3_b"], G["u3_b"]
    ones_cf, ones_rf = G["ones_cf"], G["ones_rf"]
    hqb = [sb(f"W3_hq{i}", [128, HCH * F], BF16) for i in range(2)]
    w3b = [sb(f"W3_w{i}", [128, HCH * 128], BF16) for i in range(2)]
    dwn = [sb(f"W3_d{i}", [128, CCH * F]) for i in range(2)]
    x2b = [sb(f"W3_x2{i}", [128, CCH * F]) for i in range(2)]
    scr = [sb(f"W3_scr{i}", [128, F]) for i in range(2)]
    sq = [sb(f"W3_sq{i}", [128, F], BF16) for i in range(2)]
    shad = sb("W3_shad", [128, CCH * F], BF16)
    rows = {k: sb(f"W3_r{k}", [1, F]) for k in ("m", "r", "t1", "t2")}
    mb = sb("W3_mb", [128, F])
    rb = sb("W3_rb", [128, F])
    ob = [sb("W3_o0", [128, CCH * F])]
    ev_in, ev_q, ev_w, ev_mm, ev_x2in = {}, {}, {}, {}, {}
    ev_u3, ev_out, ev_store, ev_mbrb, ev_sqdone, ev_dwnfree = {}, {}, {}, {}, {}, {}

    ts0 = [None]
    post = {}
    for j in range(NT):
        @blk.sync
        def _(sy, j=j):
            S.wait(sy, G["ev_GU_done"])
            if j >= 2:
                S.wait(sy, ev_mm[(j - 2, CCH - 1)])
            ev_in[j] = S.dma(sy, out=pcf(hqb[j % 2][:], F),
                             in_=dpcf(Hs[:, :, j * F:(j + 1) * F]), pool="din")
            if j >= 2:
                S.wait(sy, ev_out[j - 2])
            ev_x2in[j] = S.dma(sy, out=pcf(x2b[j % 2][:], F),
                               in_=dpcf(X2s[:, :, j * F:(j + 1) * F]),
                               pool="din")

        @blk.vector
        def _(v, j=j):
            S.wait(v, ev_in[j])
            if j >= 2:
                S.wait(v, ev_mm[(j - 2, CCH - 1)])
            eprev = {}
            for hch in range(HCH):
                if hch >= 2:
                    S.wait(v, eprev[hch - 2])
                e1 = S.fire(v.tensor_tensor(scr[hch % 2][:],
                                            hqb[j % 2][:, hch * F:(hch + 1) * F],
                                            cs3_b[:, j * F:(j + 1) * F],
                                            ALU.mult), "tv")
                S.wait(v, e1)
                eprev[hch] = S.fire(
                    v.tensor_scalar(hqb[j % 2][:, hch * F:(hch + 1) * F],
                                    scr[hch % 2][:], MAGIC, -MAGIC, ALU.add,
                                    ALU.add), "tv")
            ev_q[j] = eprev[HCH - 1]

        for cb in range(CCH):
            k = j * CCH + cb

            @blk.gpsimd
            def _(g, j=j, cb=cb, k=k):
                S.wait(g, G["ev_w3q_done"])
                if k >= 2:
                    S.wait(g, ev_mm[(j, cb - 2)] if cb >= 2 else
                           ev_mm[(j - 1, CCH - 2 + cb)])
                ev_w[k] = S.dma(g,
                    out=w3b[k % 2][:], in_=W3Q[cb], pool="dgp")

            @blk.tensor
            def _(t, j=j, cb=cb, k=k):
                if k == 0:
                    ts0[0] = S.c["tse"]
                S.wait(t, ev_q[j])
                S.wait(t, ev_w[k])
                nbk = min(4, CCH)
                t.wait_ge(S.s["tse"], ts0[0] + max(0, k - (nbk - 1)))
                for hch in range(HCH):
                    ins = t.matmul(ps[cb % 4][:, 0:F],
                                   w3b[k % 2][:, hch * 128:(hch + 1) * 128],
                                   hqb[j % 2][:, hch * F:(hch + 1) * F],
                                   start=(hch == 0), stop=(hch == HCH - 1))
                ev_mm[(j, cb)] = S.fire(ins, "tt")

            @blk.vector
            def _(v, j=j, cb=cb, k=k):
                S.wait(v, ev_mm[(j, cb)])
                if cb == 0 and j >= 2:
                    S.wait(v, ev_dwnfree[j - 2])
                S.fire(v.tensor_tensor(dwn[j % 2][:, cb * F:(cb + 1) * F],
                                       ps[cb % 4][:, 0:F],
                                       u3_b[:, j * F:(j + 1) * F],
                                       ALU.mult), "tse")
        ev_u3[j] = ("tse", S.c["tse"])

        if G["cfg"].dbg:
            @blk.gpsimd
            def _(g, j=j):
                S.wait(g, ev_u3[j])
                S.dma(g, out=dpcf(DWNs[:, :, j * F:(j + 1) * F]),
                      in_=pcf(dwn[j % 2][:], F), pool="dgp")

        def make_post(j, ev_u3j):
            def run():
                srcs = [dwn[j % 2][:, ch * F:(ch + 1) * F]
                        for ch in range(CCH)]
                ev_sum, ev_sq = emit_sums_sq(G, F, srcs, sq, G["ones_cb"],
                                             4, 5, ev_u3j,
                                             ev_sqdone.get(j - 1),
                                             ev_mbrb.get(j - 1), shadow=shad)
                ev_sqdone[j] = ev_sq
                ev_psf, ev_bc = ln_rows(G, F, (ev_sum, ev_sq), 4, 5, 6, 7,
                                        rows, C, ev_psfree=ev_mbrb.get(j - 1))

                @blk.vector
                def _(v, j=j, ev_bc=ev_bc):
                    S.wait(v, ev_bc)
                    v.tensor_copy(mb[:], ps[6][:, 0:F])
                    ev_mbrb[j] = S.fire(v.tensor_copy(rb[:], ps[7][:, 0:F]),
                                        "tv")
                    S.wait(v, ev_mbrb[j])
                    if j >= 1:
                        S.wait(v, ev_store[j - 1])
                    for ch in range(CCH):
                        ins = v.tensor_tensor(ob[0][:, ch * F:(ch + 1) * F],
                                              dwn[j % 2][:, ch * F:(ch + 1) * F],
                                              mb[:], ALU.subtract)
                    e1 = S.fire(ins, "tv")
                    ev_dwnfree[j] = e1
                    S.wait(v, e1)
                    for ch in range(CCH):
                        ins = v.tensor_tensor(ob[0][:, ch * F:(ch + 1) * F],
                                              ob[0][:, ch * F:(ch + 1) * F],
                                              rb[:], ALU.mult)
                    e2 = S.fire(ins, "tv")
                    S.wait(v, e2)
                    S.wait(v, ev_x2in[j])
                    ev_out[j] = S.fire(v.tensor_tensor(ob[0][:], ob[0][:],
                                                       x2b[j % 2][:],
                                                       ALU.add), "tv")

                @blk.scalar
                def _(sc, j=j):
                    S.wait(sc, ev_out[j])
                    ev_store[j] = S.dma(sc,
                        out=dpcf(outT[:, :, j * F:(j + 1) * F]),
                        in_=pcf(ob[0][:], F), pool="dsc")
            return run

        if j >= 1:
            post[j - 1]()
        post[j] = make_post(j, ev_u3[j])
    post[NT - 1]()
    G["ev_out_done"] = ev_store[NT - 1]
    _ph.close()


def _shard_inputs(cfg, inputs):
    C, TOWN, HALO, CCH = cfg.C, cfg.TOWN, cfg.HALO, cfg.CCH
    x = np.asarray(inputs["L0_attn_x"], dtype=np.float32)
    B, T, _ = x.shape
    nsl = T // TOWN

    def t_w(w):
        wt = np.ascontiguousarray(np.asarray(w, dtype=np.float32).T)
        return np.ascontiguousarray(wt.reshape(wt.shape[0] // 128, 128,
                                               wt.shape[1]))

    def vec_c(v):
        return np.ascontiguousarray(
            np.asarray(v, dtype=np.float32).reshape(-1).reshape(CCH, 128).T)

    shared = dict(
        wkT=t_w(inputs["L0_attn_Wk"]), wvT=t_w(inputs["L0_attn_Wv"]),
        wrT=t_w(inputs["L0_attn_Wr"]), woT=t_w(inputs["L0_attn_Wo"]),
        w1T=t_w(inputs["L0_ffn_w1"]), w2T=t_w(inputs["L0_ffn_w2"]),
        w3T=t_w(inputs["L0_ffn_w3"]),
        vtd=vec_c(inputs["L0_attn_time_decay"]),
        vtf=vec_c(inputs["L0_attn_time_first"]),
        vmk=vec_c(inputs["L0_attn_time_mix_k"]),
        vmv=vec_c(inputs["L0_attn_time_mix_v"]),
        vmr=vec_c(inputs["L0_attn_time_mix_r"]),
        onescb=np.ones((128, 1), ml_dtypes.bfloat16),
        onescf=np.ones((128, 1), np.float32),
        onesrf=np.ones((1, 128), np.float32),
        identfD=np.eye(128, dtype=np.float32))
    in_maps = []
    ncores = B * nsl
    for c in range(ncores):
        b, s_ = c // nsl, c % nsl
        t0 = s_ * TOWN
        if s_ == 0:
            xs = np.concatenate([np.zeros((HALO, C), np.float32),
                                 x[b, 0:TOWN]], axis=0)
            fl = np.zeros((128, 1), np.float32)
        else:
            xs = x[b, t0 - HALO:t0 + TOWN]
            fl = np.ones((128, 1), np.float32)
        m = dict(shared)
        m["xT"] = np.ascontiguousarray(xs.T).reshape(CCH, 128, cfg.TT)
        m["flagD"] = fl
        in_maps.append(m)
    return in_maps


def kernel(**inputs):
    cfg = CFG()
    x = np.asarray(inputs["L0_attn_x"])
    B, T, C = x.shape
    in_maps = _shard_inputs(cfg, inputs)
    nc = build(cfg)
    res = run_bass_kernel_spmd(nc, in_maps, core_ids=list(range(len(in_maps))))
    out = np.empty((B, T, C), np.float32)
    nsl = T // cfg.TOWN
    for c in range(len(in_maps)):
        b, s_ = c // nsl, c % nsl
        o = np.asarray(res.results[c]["outT"])
        out[b, s_ * cfg.TOWN:(s_ + 1) * cfg.TOWN, :] = \
            o.reshape(cfg.C, cfg.TOWN).T
    return out
